# version 06
# baseline (speedup 1.0000x reference)
import contextlib
import os
import numpy as np
import ml_dtypes
import concourse.bass as bass
import concourse.mybir as mybir
from concourse.bass_utils import run_bass_kernel_spmd

F32 = mybir.dt.float32
BF16 = mybir.dt.bfloat16
I32 = mybir.dt.int32
AF = mybir.ActivationFunctionType
ALU = mybir.AluOpType
AX = mybir.AxisListType
bfdt = ml_dtypes.bfloat16

B, L, E = 4, 4096, 2048
NQH, NKV, DH = 16, 2, 128
H = 8192
NT = 2048
NDC = E // 128
NHC = H // 128
NKC = L // 128
CAPS = [8, 16, 24, 32]
SM_SCALE = 1.0 / float(np.sqrt(np.float64(DH)))
EXP_BIAS = -12.0
ATANH_HALF = float(np.arctanh(0.5))
Q_TILES = {0: [0, 3, 4, 7], 1: [1, 2, 5, 6]}
DEBUG = os.environ.get("ANT_KDBG", "")


class Sched:
    ENGS = ("sync", "vector", "scalar", "tensor", "gpsimd")

    def __init__(self, nc):
        self.nc = nc
        self.items = {e: [] for e in self.ENGS}
        self.sems = {}
        self.cnt = {}

    ALIAS = {"vxq": "v1", "lxq": "g5", "lq": "g6", "tq": "t1", "sq": "s1", "vq2": "v2", "gq": "g1", "lw": "g5", "lx": "g6", "tkv": "t1", "skv": "s1", "vkr": "v2", "gkd": "g1", "vtr": "v3", "ttr": "t2", "vab": "v1", "lab": "l1", "tab": "t1", "vab2": "v2", "vab3": "v4", "sab": "s1", "vt3": "v5", "lt3": "l1", "gt3": "g7", "vmk": "v1", "se": "s1", "tsc": "t1", "tav": "t2", "ve": "v2", "vnr": "v3", "trb": "t3", "tsync": "t4", "lwo": "g6", "two": "t1", "vx1": "v1", "lx3": "l1", "gx1": "g1", "sx1": "s1", "tst": "t2", "vws": "v2", "vr1": "v1", "sr1": "s1", "tr1": "t1", "vr2": "v2", "vh": "v3", "lx4": "l1", "gxh0": "g2", "gxh1": "g3", "sh": "s2", "tst2": "t2", "vrm": "v4", "vm2": "v5", "tmx": "t3", "vmx": "v6", "gmx": "g4", "vr3": "v7", "sr2": "s3", "vr4": "v8", "tr2": "t4", "vr5": "v9", "vqz": "v10", "lx5": "l2", "gy": "g1", "ly": "l1", "vt12": "v1", "lw1": "l2", "tu": "t1", "vu1": "v2", "vps": "v3", "ssw": "s1", "gps0": "g1", "gps1": "g2", "gps2": "g3", "gps3": "g4", "ssq": "s2", "tss": "t2", "vsr": "v4", "vm3": "v5", "vr7": "v1", "sr7": "s1", "vx7": "v2", "tx7": "t1", "gx7": "g1", "vb3": "v1", "tb3": "t1", "v8q": "v2", "l8": "l1", "t8": "t2", "l83": "l2", "v8x": "v3", "l8x": "l3", "s8": "s1", "t8s": "t3", "v8s": "v4", "vr9": "v1", "sr9": "s1", "tr9": "t1", "vr10": "v2", "vo": "v3", "go0": "l4", "go1": "l5", "gid": "g4", "gx2": "g5", "lx9": "l1", "g8": "g6"}

    def sem(self, name):
        name = self.ALIAS.get(name, name)
        if name not in self.sems:
            self.sems[name] = self.nc.alloc_semaphore(name=name)
            self.cnt[name] = 0
        return self.sems[name]

    def val(self, name):
        name = self.ALIAS.get(name, name)
        self.sem(name)
        return self.cnt[name]

    def op(self, eng, fn, wait=(), inc=(), tag=""):
        wait = [(self.ALIAS.get(s, s), v) for s, v in wait if v > 0]
        inc = [(self.ALIAS.get(s, s), n) for s, n in inc]
        if getattr(self, "epoch", 0) > 0:
            wait = [("ph", self.epoch)] + wait
        for s, _ in wait:
            self.sem(s)
        for s, n in inc:
            self.sem(s)
            self.cnt[s] += n
        self.items[eng].append((fn, wait, list(inc), tag))

    def barrier(self):
        waits = [(n, v) for n, v in self.cnt.items()
                 if v > 0 and n not in ("done", "ph")]
        self.epoch = getattr(self, "epoch", 0)
        for eng in self.ENGS:
            self.op(eng, lambda e: e.nop(), wait=waits, inc=[("ph", 1)],
                    tag=f"barrier{self.epoch}")
        self.epoch += 5

    def dma(self, eng, out, in_, wait=(), inc=(), tag="", slow=False):
        wait = list(wait) + [(sname, self.val(sname)) for sname, _ in inc]
        if slow:
            self.op(eng, lambda e, o=out, i=in_: e.dma_start(
                out=o, in_=i, allow_slow_non_contiguous=True),
                wait=wait, inc=inc, tag=tag)
        else:
            self.op(eng, lambda e, o=out, i=in_: e.dma_start(out=o, in_=i),
                    wait=wait, inc=inc, tag=tag)

    def emit(self, block):
        for eng in self.ENGS:
            def mk(eng):
                def body(e):
                    for fn, wait, inc, tag in self.items[eng]:
                        for s, v in wait:
                            e.wait_ge(self.sems[s], v)
                        ins = fn(e)
                        if inc:
                            assert ins is not None, f"inc on None ({tag})"
                            for s, n in inc:
                                ins = ins.then_inc(self.sems[s], n)
                    if "done" in self.sems:
                        e.wait_ge(self.sems["done"], self.cnt["done"])
                return body
            getattr(block, eng)(mk(eng))


def rope_tables():
    import jax
    import jax.numpy as jnp
    with jax.default_device(jax.devices("cpu")[0]):
        half = DH // 2
        inv_freq = 10000.0 ** (-jnp.arange(half, dtype=jnp.float32) / half)
        pos = jnp.arange(L, dtype=jnp.float32)
        ang = pos[:, None] * inv_freq[None, :]
        cos = np.asarray(jnp.cos(ang))
        sin = np.asarray(jnp.sin(ang))
    COS = np.concatenate([cos.T, cos.T], axis=0)
    SINP = np.concatenate([-sin.T, sin.T], axis=0)
    return np.ascontiguousarray(COS), np.ascontiguousarray(SINP)


def tile_w(wT):
    D, O = wT.shape
    t = wT.reshape(D // 128, 128, O // 128, 128)
    return np.ascontiguousarray(t.transpose(2, 1, 0, 3).reshape(O // 128, 128, D))


def prep_inputs(inp):
    x = np.asarray(inp["L0_attn_x"], dtype=np.float32)
    COS, SINP = rope_tables()
    T = lambda a: np.ascontiguousarray(np.asarray(a, np.float32).T)
    wqt = tile_w(T(inp["L0_attn_wq"]))
    wkt = tile_w(T(inp["L0_attn_wk"]))
    wvt = tile_w(T(inp["L0_attn_wv"]))
    wot = tile_w(T(inp["L0_attn_wo"]))
    w1t = tile_w(T(inp["L0_ffn_w1"]))
    w2t = tile_w(T(inp["L0_ffn_w2"]))
    w3t = tile_w(T(inp["L0_ffn_w3"]))
    rs = lambda k, n: np.ascontiguousarray(
        np.asarray(inp[k], np.float32).reshape(n, 128).T)
    bq, bk, bv = rs("L0_attn_bq", 16), rs("L0_attn_bk", 2), rs("L0_attn_bv", 2)
    g1, g2, g3 = rs("L0_ffn_g1", 16), rs("L0_ffn_g2", 16), rs("L0_ffn_g3", 64)
    iden = np.eye(128, dtype=np.float32)
    cst = np.tile(np.array([[SM_SCALE, EXP_BIAS, -0.5, 0.5]], dtype=np.float32),
                  (128, 1))
    shared = dict(wqt=wqt, wkt=wkt, wvt=wvt, wot=wot, w1t=w1t, w2t=w2t,
                  w3t=w3t, bq=bq, bk=bk, bv=bv, g1=g1, g2=g2, g3=g3,
                  iden=iden, cst=cst, cosk=COS, sink=SINP)
    in_maps = []
    for c in range(8):
        b, par = c >> 1, c & 1
        tiles = Q_TILES[par]
        qcols = np.concatenate([np.arange(g * 512, (g + 1) * 512) for g in tiles])
        xT = np.ascontiguousarray(x[b].T)
        m = dict(shared)
        m.update(xt=xT,
                 xq=np.ascontiguousarray(xT[:, qcols]),
                 cosq=np.ascontiguousarray(COS[:, qcols]),
                 sinq=np.ascontiguousarray(SINP[:, qcols]))
        thr = np.zeros((128, 32), dtype=np.float32)
        for s in range(4):
            q0 = tiles[s] * 512
            for i in range(8):
                j = CAPS[s] - 8 + i
                thr[:, s * 8 + i] = float(q0 - j * 128)
        m["thr"] = thr
        in_maps.append(m)
    return in_maps


def build(phases=99):
    nc = bass.Bass("TRN2", target_bir_lowering=False,
                   detect_race_conditions=False)
    A = lambda name, shape, dt=F32, kind="ExternalInput": \
        nc.dram_tensor(name, shape, dt, kind=kind).ap()

    xt = A("xt", [E, L]); xq = A("xq", [E, NT])
    cosk = A("cosk", [128, L]); sink = A("sink", [128, L])
    cosq = A("cosq", [128, NT]); sinq = A("sinq", [128, NT])
    wqt = A("wqt", [16, 128, 2048]); wkt = A("wkt", [2, 128, 2048])
    wvt = A("wvt", [2, 128, 2048]); wot = A("wot", [16, 128, 2048])
    w1t = A("w1t", [64, 128, 2048]); w2t = A("w2t", [64, 128, 2048])
    w3t = A("w3t", [16, 128, 8192])
    bq = A("bq", [128, 16]); bk = A("bk", [128, 2]); bv = A("bv", [128, 2])
    g1 = A("g1", [128, 16]); g2 = A("g2", [128, 16]); g3 = A("g3", [128, 64])
    iden = A("iden", [128, 128]); cst = A("cst", [128, 4])
    thr = A("thr", [128, 32])
    out = A("out", [E, NT], kind="ExternalOutput")

    x1pre_hbm = A("x1pre_hbm", [E, NT], kind="Internal")
    xhat_hbm = A("xhat_hbm", [E, NT], kind="Internal")
    x2_hbm = A("x2_hbm", [E, NT], BF16, kind="Internal")
    y_hbm = A("y_hbm", [2, E, NT], BF16, kind="Internal")
    psi_hbm = A("psi_hbm", [4, H, 512], BF16, kind="Internal")
    t3_hbm = A("t3_hbm", [16, 128, 8192], BF16, kind="Internal")
    m2x_hbm = A("m2x_hbm", [128, 16], BF16, kind="Internal")

    dbg = {}
    if DEBUG:
        for k, shp in {"kq": [256, L], "vv": [NKC * 128, 256],
                       "qrope": [E, NT], "attno": [E, NT], "x1pre": [E, NT],
                       "xhat": [E, NT],
                       "y1": [E, NT], "y2": [E, NT], "rows": [8, NT],
                       "srow": [4, 4], "x2": [E, NT]}.items():
            dbg[k] = A("d_" + k, shp, kind="ExternalOutput")

    gctx = contextlib.ExitStack()
    SB = lambda c, name, shape, dt=F32: c.enter_context(nc.sbuf_tensor(name, shape, dt))
    PS = lambda c, name, shape, dt=F32: c.enter_context(nc.psum_tensor(name, shape, dt))

    with gctx:
        cstt = SB(gctx, "cstt", [128, 4])
        idb = SB(gctx, "idb", [128, 128], BF16)
        thrt = SB(gctx, "thrt", [128, 32])
        onesb = SB(gctx, "onesb", [128, 1], BF16)
        onesr = SB(gctx, "onesr", [1, 128])
        bqt = SB(gctx, "bqt", [128, 16]); bkt = SB(gctx, "bkt", [128, 2])
        bvt = SB(gctx, "bvt", [128, 2])
        g1t = SB(gctx, "g1t", [128, 16]); g2t = SB(gctx, "g2t", [128, 16])
        g3t = SB(gctx, "g3t", [128, 64])
        g1sq = SB(gctx, "g1sq", [128, 16]); g2sq = SB(gctx, "g2sq", [128, 16])
        g3sq = SB(gctx, "g3sq", [128, 64])
        cells = SB(gctx, "cells", [1, 64])
        csnap = SB(gctx, "csnap", [1, 32])
        C = lambda i: cells[0:1, i:i + 1]
        C_S1, C_S2, C_S3, C_A1, C_A2, C_A3, C_TH1, C_TH2, C_TH3, C_T0, C_T1, \
            C_T2 = range(12)
        thcol = SB(gctx, "thcol", [128, 8])

        nc.const_aps.register(F32, SM_SCALE, cstt[:, 0:1])
        nc.const_aps.register(F32, EXP_BIAS, cstt[:, 1:2])
        nc.const_aps.register(F32, -0.5, cstt[:, 2:3])
        nc.const_aps.register(F32, 0.5, cstt[:, 3:4])

        S = Sched(nc)
        S.sem("done")
        ld = "ld"

        for dst, src in [(cstt[:], cst), (thrt[:], thr),
                         (bqt[:], bq), (bkt[:], bk), (bvt[:], bv),
                         (g1t[:], g1), (g2t[:], g2), (g3t[:], g3)]:
            S.dma("sync", dst, src, inc=[(ld, 16)])
        p0_ld = S.val(ld)
        S.dma("gpsimd", idb[:], iden, inc=[("gid", 16)])
        gid_ld = S.val("gid")
        S.op("vector", lambda e: e.memset(onesb[:], 1.0), inc=[("v0", 1)])
        S.op("vector", lambda e: e.memset(onesr[:], 1.0), inc=[("v0", 1)])
        S.op("vector", lambda e: e.tensor_mul(g1sq[:], g1t[:], g1t[:]),
             wait=[(ld, p0_ld)], inc=[("v0", 1)])
        S.op("vector", lambda e: e.tensor_mul(g2sq[:], g2t[:], g2t[:]),
             inc=[("v0", 1)])
        S.op("vector", lambda e: e.tensor_mul(g3sq[:], g3t[:], g3t[:]),
             inc=[("v0", 1)])
        v0_base = S.val("v0")

        prctx = contextlib.ExitStack()
        prows = SB(prctx, "prows", [1, 4 * NT])
        PR = lambda i: prows[0:1, i * NT:(i + 1) * NT]
        PA, PB, PC, PD = 0, 1, 2, 3

        actx = contextlib.ExitStack()
        arena = SB(actx, "arena", [128, 17 * NT], BF16)
        ASL = lambda i: arena[:, i * NT:(i + 1) * NT]
        D_iota = SB(actx, "D_iota", [128, 512])
        S.op("gpsimd", lambda e: e.iota(
            D_iota[:], pattern=[[1, 512]], base=0, channel_multiplier=-1,
            allow_small_or_imprecise_dtypes=True), inc=[("g0", 1)])

        with contextlib.ExitStack() as p3a:
            xqb = SB(p3a, "xqb", [128, 16 * NT], BF16)
            for dc in range(16):
                S.dma("gpsimd", xqb[:, dc * NT:(dc + 1) * NT],
                      xq[dc * 128:(dc + 1) * 128, :], inc=[("lxq", 16)])
            xq_ld = S.val("lxq")
            ckq = SB(p3a, "ckq", [128, NT]); skq = SB(p3a, "skq", [128, NT])
            S.dma("sync", ckq[:], cosq, inc=[(ld, 16)])
            S.dma("sync", skq[:], sinq, inc=[(ld, 16)])
            ld_q = S.val(ld)
            wqb = SB(p3a, "wqb", [128, 2048], BF16)
            qf = SB(p3a, "qf", [128, NT])
            qsw = SB(p3a, "qsw", [128, NT // 2])
            pq = PS(p3a, "pq", [128, 2 * 512])
            for h in range(16):
                S.dma("gpsimd", wqb[:], wqt[h],
                      wait=[("tq", S.val("tq"))], inc=[("lq", 16)])
                for sl in range(4):
                    def qmm(e, sl=sl):
                        last = None
                        for dc in range(16):
                            last = e.matmul(
                                pq[:, (sl % 2) * 512:(sl % 2 + 1) * 512],
                                wqb[:, dc * 128:(dc + 1) * 128],
                                xqb[:, dc * NT + sl * 512:dc * NT + (sl + 1) * 512],
                                start=(dc == 0), stop=(dc == 15))
                        return last
                    S.op("tensor", qmm,
                         wait=[("lq", S.val("lq")), ("lxq", xq_ld),
                               ("sq", max(0, S.val("sq") - 1))] +
                              ([("vq2", 6 * h)] if sl == 0 else []),
                         inc=[("tq", 1)], tag=f"qmm{h}_{sl}")
                    S.op("scalar", lambda e, sl=sl, h=h: e.activation(
                        qf[:, sl * 512:(sl + 1) * 512],
                        pq[:, (sl % 2) * 512:(sl % 2 + 1) * 512], AF.Identity,
                        bias=bqt[:, h:h + 1]),
                        wait=[("tq", S.val("tq")), (ld, p0_ld)], inc=[("sq", 1)])
                sq_h = S.val("sq")
                for qh in range(2):
                    cs = slice(qh * (NT // 2), (qh + 1) * (NT // 2))
                    S.dma("gpsimd", qsw[0:64, :], qf[64:128, cs],
                          wait=[("sq", sq_h), ("vq2", S.val("vq2"))],
                          inc=[("gq", 16)])
                    S.dma("gpsimd", qsw[64:128, :], qf[0:64, cs],
                          inc=[("gq", 16)])
                    S.op("vector", lambda e, cs=cs: e.tensor_mul(
                        qf[:, cs], qf[:, cs], ckq[:, cs]),
                        wait=[("sq", sq_h), (ld, ld_q), ("gq", S.val("gq"))],
                        inc=[("vq2", 1)])
                    S.op("vector", lambda e, cs=cs: e.tensor_mul(
                        qsw[:], qsw[:], skq[:, cs]), inc=[("vq2", 1)])
                    S.op("vector", lambda e, h=h, cs=cs: e.tensor_add(
                        ASL(h)[:, cs], qf[:, cs], qsw[:]),
                        inc=[("vq2", 1)], tag=f"qrope{h}_{qh}")
        q_done = S.val("vq2")
        S.barrier()

        if DEBUG and phases == 2:
            with contextlib.ExitStack() as pd:
                qdf = SB(pd, "qdf", [128, NT])
                for h in range(16):
                    S.op("vector", lambda e, h=h: e.tensor_copy(qdf[:], ASL(h)),
                         wait=[("vq2", q_done), ("od", S.val("od"))],
                         inc=[("vd", 1)])
                    S.dma("sync", dbg["qrope"][h * 128:(h + 1) * 128, :], qdf[:],
                          wait=[("vd", S.val("vd"))], inc=[("od", 16)])
                S.dma("sync", out[0:128, :], xq[0:128, :], inc=[("od", 16)])
                S.op("sync", lambda e: e.nop(), wait=[("od", S.val("od"))],
                     inc=[("done", 1)])
            actx.close(); prctx.close()
            with nc.Block() as block:
                S.emit(block)
            return nc

        kvctx = contextlib.ExitStack()
        kb = SB(kvctx, "kb", [128, 2 * L], BF16)
        vtok = SB(kvctx, "vtok", [128, NKC * 256], BF16)

        with contextlib.ExitStack() as p1:
            wkb = SB(p1, "wkb", [128, 16 * 256], BF16)
            wvb = SB(p1, "wvb", [128, 16 * 256], BF16)
            xkb = SB(p1, "xkb", [128, 16 * 512], BF16)
            kfh = SB(p1, "kfh", [128, NT])
            vbfh = SB(p1, "vbfh", [128, NT], BF16)
            ckh = SB(p1, "ckh", [128, NT]); skh = SB(p1, "skh", [128, NT])
            ksw = SB(p1, "ksw", [128, NT])
            pkv = PS(p1, "pkv", [128, 2 * 512])
            ptr = PS(p1, "ptr", [128, 128], BF16)
            for wsrc, wdst in [(wkt, wkb), (wvt, wvb)]:
                for oc in range(2):
                    def _wdst_ap(wdst=wdst, oc=oc):
                        return wdst[:].rearrange("p (a b) -> p a b", b=128)[:, :, :]
                    for dc in range(16):
                        S.dma("gpsimd",
                              wdst[:, dc * 256 + oc * 128:dc * 256 + oc * 128 + 128],
                              wsrc[oc, :, dc * 128:(dc + 1) * 128],
                              inc=[("lw", 16)])
            v0_kv = S.val("vw")
            for kvh in range(2):
                for half in range(2):
                    for sl4 in range(4):
                        sl = half * 4 + sl4
                        for dc in range(16):
                            S.dma("gpsimd", xkb[:, dc * 512:(dc + 1) * 512],
                                  xt[dc * 128:(dc + 1) * 128, sl * 512:(sl + 1) * 512],
                                  wait=([("tkv", S.val("tkv"))] if dc == 0 else []),
                                  inc=[("lx", 16)])
                        def mm_slab(e, kvh=kvh):
                            last = None
                            for w2 in range(2):
                                wsel = wkb if w2 == 0 else wvb
                                for dc in range(16):
                                    lt = wsel[:, dc * 256 + kvh * 128:dc * 256 + kvh * 128 + 128]
                                    last = e.matmul(
                                        pkv[:, w2 * 512:(w2 + 1) * 512], lt,
                                        xkb[:, dc * 512:(dc + 1) * 512],
                                        start=(dc == 0), stop=(dc == 15))
                            return last
                        S.op("tensor", mm_slab,
                             wait=[("lx", S.val("lx")), ("lw", S.val("lw")),
                                   ("skv", S.val("skv"))],
                             inc=[("tkv", 1)], tag=f"mmkv{kvh}_{sl}")
                        def evac(e, kvh=kvh, sl4=sl4):
                            e.activation(kfh[:, sl4 * 512:(sl4 + 1) * 512],
                                         pkv[:, 0:512], AF.Identity,
                                         bias=bkt[:, kvh:kvh + 1])
                            return e.activation(vbfh[:, sl4 * 512:(sl4 + 1) * 512],
                                                pkv[:, 512:1024], AF.Identity,
                                                bias=bvt[:, kvh:kvh + 1])
                        S.op("scalar", evac,
                             wait=[("tkv", S.val("tkv")), (ld, p0_ld),
                                   ("vkr", S.val("vkr")),
                                   ("vtr", S.val("vtr"))],
                             inc=[("skv", 1)], tag=f"evkv{kvh}_{sl}")
                    skv_h = S.val("skv")
                    S.dma("gpsimd", ksw[0:64, :], kfh[64:128, :],
                          wait=[("skv", skv_h), ("vkr", S.val("vkr"))],
                          inc=[("gkd", 16)])
                    S.dma("gpsimd", ksw[64:128, :], kfh[0:64, :],
                          inc=[("gkd", 16)])
                    cs = slice(half * NT, (half + 1) * NT)
                    S.dma("sync", ckh[:], cosk[:, cs],
                          wait=[("vkr", S.val("vkr"))], inc=[(ld, 16)])
                    S.dma("sync", skh[:], sink[:, cs], inc=[(ld, 16)])
                    ld_h = S.val(ld)
                    S.op("vector", lambda e: e.tensor_mul(kfh[:], kfh[:], ckh[:]),
                         wait=[("skv", skv_h), (ld, ld_h),
                               ("gkd", S.val("gkd"))], inc=[("vkr", 1)])
                    S.op("vector", lambda e: e.tensor_mul(ksw[:], ksw[:], skh[:]),
                         inc=[("vkr", 1)])
                    S.op("vector", lambda e, kvh=kvh, cs=cs: e.tensor_add(
                        kb[:, kvh * L:(kvh + 1) * L][:, cs], kfh[:], ksw[:]),
                        inc=[("vkr", 1)], tag=f"ropek{kvh}_{half}")
                    for kc4 in range(16):
                        kc = half * 16 + kc4
                        S.op("tensor", lambda e, kc4=kc4: e.transpose(
                            ptr[:], vbfh[:, kc4 * 128:(kc4 + 1) * 128], idb[:]),
                            wait=[("skv", skv_h), ("gid", gid_ld),
                                  ("vtr", S.val("vtr"))], inc=[("ttr", 1)])
                        S.op("vector", lambda e, kc=kc, kvh=kvh: e.tensor_copy(
                            vtok[:, kc * 256 + kvh * 128:kc * 256 + (kvh + 1) * 128],
                            ptr[:]), wait=[("ttr", S.val("ttr"))],
                            inc=[("vtr", 1)])
        kv_done = [("vkr", S.val("vkr")), ("vtr", S.val("vtr"))]
        S.barrier()

        if DEBUG and phases == 1:
            with contextlib.ExitStack() as pd:
                kqf = SB(pd, "kqf", [128, L])
                for kvh in range(2):
                    S.op("vector", lambda e, kvh=kvh: e.tensor_copy(
                        kqf[:], kb[:, kvh * L:(kvh + 1) * L]),
                        wait=kv_done + [("od", S.val("od"))], inc=[("vd", 1)])
                    S.dma("sync", dbg["kq"][kvh * 128:(kvh + 1) * 128, :], kqf[:],
                          wait=[("vd", S.val("vd"))], inc=[("od", 16)])
                vvf = SB(pd, "vvf", [128, 256])
                for kc in range(NKC):
                    S.op("vector", lambda e, kc=kc: e.tensor_copy(
                        vvf[:], vtok[:, kc * 256:(kc + 1) * 256]),
                        wait=kv_done + [("od", S.val("od"))], inc=[("vd", 1)])
                    S.dma("sync", dbg["vv"][kc * 128:(kc + 1) * 128, :], vvf[:],
                          wait=[("vd", S.val("vd"))], inc=[("od", 16)])
                S.dma("sync", out[0:128, :], xq[0:128, :], inc=[("od", 16)])
                S.op("sync", lambda e: e.nop(), wait=[("od", S.val("od"))],
                     inc=[("done", 1)])
            kvctx.close(); actx.close(); prctx.close()
            with nc.Block() as block:
                S.emit(block)
            return nc

        with contextlib.ExitStack() as p2:
            wsl = [SB(p2, f"wsl{i}", [128, 2048]) for i in range(2)]
            wab = [SB(p2, f"wab{i}", [128, 2048], BF16) for i in range(2)]
            pab = PS(p2, "pab", [128, 512])
            abrow = SB(p2, "abrow", [1, 3 * 512])
            S.op("vector", lambda e: e.memset(abrow[:], 0.0), inc=[("vab", 1)])
            slabs = []
            for wi, wt_ap in enumerate([w1t, w2t]):
                for hc in range(64):
                    slabs.append((wi, wt_ap[hc]))
            for oc in range(16):
                for q in range(4):
                    slabs.append((2, w3t[oc, :, q * 2048:(q + 1) * 2048]))
            for si, (wi, src) in enumerate(slabs):
                bi = si % 2
                S.dma("sync", wsl[bi][:], src,
                      wait=[("vab", max(0, S.val("vab") - 2))], inc=[("lab", 16)])
                S.op("scalar", lambda e, bi=bi: e.activation(
                    wab[bi][:], wsl[bi][:], AF.Abs),
                    wait=[("lab", S.val("lab")),
                          ("tab", max(0, S.val("tab") - 1))],
                    inc=[("vab", 1)], tag=f"abs{si}")
                def absmm(e, bi=bi):
                    last = None
                    for s4 in range(4):
                        last = e.matmul(pab[0:1, 0:512], onesb[:],
                                        wab[bi][:, s4 * 512:(s4 + 1) * 512],
                                        start=(s4 == 0), stop=(s4 == 3))
                    return last
                S.op("tensor", absmm,
                     wait=[("vab", S.val("vab")), ("v0", v0_base),
                           ("vab2", S.val("vab2"))], inc=[("tab", 1)])
                S.op("vector", lambda e, wi=wi: e.scalar_tensor_tensor(
                    abrow[0:1, wi * 512:(wi + 1) * 512], pab[0:1, 0:512], 1.0,
                    abrow[0:1, wi * 512:(wi + 1) * 512], ALU.mult, ALU.add),
                    wait=[("tab", S.val("tab"))], inc=[("vab2", 1)],
                    tag=f"abr{si}")
            def fin_rows(e):
                for wi in range(3):
                    e.reduce_sum(C(C_T0 + wi), abrow[0:1, wi * 512:(wi + 1) * 512],
                                 axis=AX.X)
                e.tensor_scalar_mul(C(C_S1), C(C_T0), 1.0 / (H * E))
                e.tensor_scalar_mul(C(C_S2), C(C_T1), 1.0 / (H * E))
                return e.tensor_scalar_mul(C(C_S3), C(C_T2), 1.0 / (H * E))
            S.op("vector", fin_rows, wait=[("vab2", S.val("vab2"))],
                 inc=[("vab3", 1)])
            def fin2a(e):
                last = None
                for i, (cs, cth) in enumerate([(C_S1, C_TH1), (C_S2, C_TH2),
                                               (C_S3, C_TH3)]):
                    e.tensor_scalar(C(cth), C(cs), 1e-8, ATANH_HALF, ALU.add,
                                    ALU.mult)
                    e.tensor_scalar(C(C_T0 + 3 + i), C(cs), 1.0, None, ALU.add)
                    last = e.tensor_scalar(C(C_T0 + 6 + i), C(cs), -1.0, 1.0,
                                           ALU.mult, ALU.add)
                return last
            S.op("vector", fin2a, inc=[("vab3", 1)])
            def fin2b(e):
                last = None
                for i in range(3):
                    last = e.reciprocal(C(C_T0 + 9 + i), C(C_T0 + 6 + i))
                return last
            S.op("vector", fin2b, inc=[("vab3", 1)])
            def fin2c(e):
                last = None
                for i, ca in enumerate((C_A1, C_A2, C_A3)):
                    last = e.tensor_mul(C(ca), C(C_T0 + 3 + i), C(C_T0 + 9 + i))
                return last
            S.op("vector", fin2c, wait=[("vab3", S.val("vab3"))],
                 inc=[("vab3", 1)])
            S.op("vector", lambda e: e.tensor_copy(csnap[0:1, 0:16],
                                                    cells[0:1, 0:16]),
                 wait=[("vab3", S.val("vab3"))], inc=[("vab3", 1)])
            def fin3(e):
                last = None
                for ca in (C_A1, C_A2, C_A3):
                    last = e.activation(C(ca), C(ca), AF.Ln)
                return last
            S.op("scalar", fin3, wait=[("vab3", S.val("vab3"))], inc=[("sab", 1)])
            S.op("vector", lambda e: e.tensor_copy(csnap[0:1, 16:32],
                                                   cells[0:1, 0:16]),
                 wait=[("sab", S.val("sab"))], inc=[("vab3", 1)])
            def fin4(e):
                last = None
                for ca in (C_A1, C_A2, C_A3):
                    last = e.tensor_scalar_mul(C(ca), C(ca), 0.5)
                return last
            S.op("vector", fin4, wait=[("sab", S.val("sab"))], inc=[("vab3", 1)])
            pbc1 = PS(p2, "pbc1", [128, 8])
            def thbc_mm(e):
                last = None
                for wi, cth in enumerate((C_TH1, C_TH2, C_TH3)):
                    last = e.matmul(pbc1[:, 2 * wi:2 * wi + 1], onesr[:], C(cth),
                                    start=True, stop=True)
                return last
            S.op("tensor", thbc_mm, wait=[("vab3", S.val("vab3")),
                                          ("v0", v0_base)], inc=[("tab", 1)])
            def thbc_fin(e):
                e.tensor_copy(thcol[:, 0:1], pbc1[:, 0:1])
                e.tensor_copy(thcol[:, 2:3], pbc1[:, 2:3])
                e.tensor_copy(thcol[:, 4:5], pbc1[:, 4:5])
                e.tensor_scalar_mul(thcol[:, 1:2], pbc1[:, 0:1], -1.0)
                e.tensor_scalar_mul(thcol[:, 3:4], pbc1[:, 2:3], -1.0)
                return e.tensor_scalar_mul(thcol[:, 5:6], pbc1[:, 4:5], -1.0)
            S.op("vector", thbc_fin, wait=[("tab", S.val("tab"))],
                 inc=[("vab3", 1)])
        p2_done = [("vab3", S.val("vab3"))]
        S.barrier()

        with contextlib.ExitStack() as p2b:
            w3sl = SB(p2b, "w3sl", [128, 8192])
            w3tp = SB(p2b, "w3tp", [128, 8192], BF16)
            t3b = SB(p2b, "t3b", [128, 8192], BF16)
            for oc in range(16):
                S.dma("sync", w3sl[:], w3t[oc],
                      wait=[("vt3", S.val("vt3"))], inc=[("lt3", 16)])
                S.op("vector", lambda e: e.tensor_scalar(
                    w3tp[:], w3sl[:], thcol[:, 4:5], None, ALU.is_ge),
                    wait=[("lt3", S.val("lt3")), ("gt3", S.val("gt3"))] + p2_done,
                    inc=[("vt3", 1)])
                S.op("vector", lambda e: e.scalar_tensor_tensor(
                    t3b[:], w3sl[:], thcol[:, 5:6], w3tp[:], ALU.is_lt,
                    ALU.subtract), inc=[("vt3", 1)], tag=f"t3_{oc}")
                S.dma("gpsimd", t3_hbm[oc], t3b[:],
                      wait=[("vt3", S.val("vt3"))], inc=[("gt3", 16)])

        S.barrier()
        with contextlib.ExitStack() as p3b:
            masks = SB(p3b, "masks", [128, 32 * 512], BF16)
            def genmasks(e):
                last = None
                for u in range(32):
                    last = e.tensor_scalar(
                        masks[:, u * 512:(u + 1) * 512], D_iota[:],
                        thrt[:, u:u + 1], 0.0, ALU.add, ALU.is_ge)
                return last
            S.op("vector", genmasks, wait=[("g0", 1), (ld, p0_ld)],
                 inc=[("vmk", 1)])
            ebuf = [SB(p3b, f"ebuf{i}", [128, 2 * 512], BF16) for i in range(2)]
            esum = SB(p3b, "esum", [128, 512], BF16)
            rrow = SB(p3b, "rrow", [1, 512])
            rbc = SB(p3b, "rbc", [128, 512])
            psc = [PS(p3b, f"psc{i}", [128, 2 * 512]) for i in range(2)]
            pot = [PS(p3b, f"pot{i}", [128, 512]) for i in range(2)]
            pdn = PS(p3b, "pdn", [1, 512])
            prb = PS(p3b, "prb", [128, 512])
            vmk = S.val("vmk")
            vnr0 = S.val("vnr")
            unit = 0
            for h in range(16):
                kvh = h // 8
                qsl = ASL(h)
                outsl = ASL((h + 16) % 17)
                for s in range(4):
                    cap = CAPS[s]
                    ngr = cap // 2
                    q0, q1 = s * 512, (s + 1) * 512
                    upp = unit % 2
                    tsc_after = {}

                    def emit_smm(g):
                        pbi = g % 2
                        def smm(e, g=g, pbi=pbi, kvh=kvh, qsl=qsl, q0=q0, q1=q1):
                            last = None
                            for j in range(2):
                                kc = g * 2 + j
                                last = e.matmul(
                                    psc[pbi][:, j * 512:(j + 1) * 512],
                                    kb[:, kvh * L + kc * 128:kvh * L + (kc + 1) * 128],
                                    qsl[:, q0:q1], start=True, stop=True)
                            return last
                        S.op("tensor", smm,
                             wait=[("se", S.val("se")), ("vq2", q_done)] +
                                  (kv_done if h == 0 and s == 0 and g == 0 else []),
                             inc=[("tsc", 1)], tag=f"smm{h}_{s}_{g}")
                        tsc_after[g] = S.val("tsc")

                    emit_smm(0)
                    for g in range(ngr):
                        pbi = g % 2
                        if g + 1 < ngr:
                            emit_smm(g + 1)
                        S.op("scalar", lambda e, pbi=pbi: e.activation(
                            ebuf[pbi][:], psc[pbi][:], AF.Exp, bias=EXP_BIAS,
                            scale=SM_SCALE),
                            wait=[("tsc", tsc_after[g]),
                                  ("tav", max(0, S.val("tav") - 1))],
                            inc=[("se", 1)], tag=f"exp{h}_{s}_{g}")
                        def vstep(e, g=g, pbi=pbi, s=s, cap=cap):
                            for j in range(2):
                                kc = g * 2 + j
                                if kc >= cap - 8:
                                    u = s * 8 + (kc - (cap - 8))
                                    e.tensor_mul(
                                        ebuf[pbi][:, j * 512:(j + 1) * 512],
                                        ebuf[pbi][:, j * 512:(j + 1) * 512],
                                        masks[:, u * 512:(u + 1) * 512])
                            return e.tensor_add(esum[:], ebuf[pbi][:, 0:512],
                                                ebuf[pbi][:, 512:1024])
                        S.op("vector", vstep,
                             wait=[("se", S.val("se")), ("vmk", vmk),
                                   ("tav", S.val("tav"))],
                             inc=[("ve", 1)], tag=f"vm{h}_{s}_{g}")
                        def avmm(e, g=g, pbi=pbi, kvh=kvh, ngr=ngr, upp=upp):
                            e.matmul(pdn[0:1, :], onesb[:], esum[:],
                                     start=(g == 0), stop=(g == ngr - 1))
                            last = None
                            for j in range(2):
                                kc = g * 2 + j
                                last = e.matmul(
                                    pot[upp][:], vtok[:, kc * 256 + kvh * 128:
                                                      kc * 256 + (kvh + 1) * 128],
                                    ebuf[pbi][:, j * 512:(j + 1) * 512],
                                    start=(g == 0 and j == 0),
                                    stop=(g == ngr - 1 and j == 1))
                            return last
                        av_wait = [("ve", S.val("ve"))]
                        if g == 0 and unit >= 2:
                            av_wait.append(("vnr", vnr0 + 3 * (unit - 1)))
                        S.op("tensor", avmm, wait=av_wait,
                             inc=[("tav", 1)], tag=f"av{h}_{s}_{g}")
                    S.op("vector", lambda e: e.reciprocal(rrow[:], pdn[0:1, :]),
                         wait=[("tav", S.val("tav"))], inc=[("vnr", 1)])
                    S.op("tensor", lambda e: e.matmul(
                        prb[:], onesr[:], rrow[:], start=True, stop=True),
                        wait=[("vnr", S.val("vnr"))], inc=[("trb", 1)])
                    S.op("vector", lambda e: e.tensor_copy(rbc[:], prb[:]),
                         wait=[("trb", S.val("trb"))], inc=[("vnr", 1)])
                    S.op("vector", lambda e, outsl=outsl, q0=q0, q1=q1, upp=upp:
                         e.tensor_mul(outsl[:, q0:q1], pot[upp][:], rbc[:]),
                         inc=[("vnr", 1)], tag=f"on{h}_{s}")
                    unit += 1
        att_done = S.val("vnr")
        S.barrier()
        kvctx.close()

        if DEBUG and phases == 4:
            with contextlib.ExitStack() as pd:
                adf = SB(pd, "adf", [128, NT])
                for h in range(16):
                    S.op("vector", lambda e, h=h: e.tensor_copy(
                        adf[:], ASL((h + 16) % 17)),
                        wait=[("vnr", att_done), ("od", S.val("od"))],
                        inc=[("vd", 1)])
                    S.dma("sync", dbg["attno"][h * 128:(h + 1) * 128, :], adf[:],
                          wait=[("vd", S.val("vd"))], inc=[("od", 16)])
                S.dma("sync", out[0:128, :], xq[0:128, :], inc=[("od", 16)])
                S.op("sync", lambda e: e.nop(), wait=[("od", S.val("od"))],
                     inc=[("done", 1)])
            actx.close(); prctx.close()
            with nc.Block() as block:
                S.emit(block)
            return nc

        with contextlib.ExitStack() as p4:
            wob = SB(p4, "wob", [128, 2048], BF16)
            xstg3 = [SB(p4, f"xstg3{i}", [128, NT]) for i in range(2)]
            x1c = SB(p4, "x1c", [128, NT])
            x1bf = SB(p4, "x1bf", [128, NT], BF16)
            sqbf = SB(p4, "sqbf", [128, NT], BF16)
            pwo = [PS(p4, f"pwo{i}", [128, 512]) for i in range(2)]
            pst = PS(p4, "pst", [1, 512])
            S.op("vector", lambda e: e.memset(PR(PA), 0.0), inc=[("vws", 1)])
            S.op("vector", lambda e: e.memset(PR(PB), 0.0), inc=[("vws", 1)])
            for oc in range(16):
                bi = oc % 2
                S.dma("gpsimd", wob[:], wot[oc],
                      wait=[("two", S.val("two"))], inc=[("lwo", 16)])
                S.dma("sync", xstg3[bi][:], xq[oc * 128:(oc + 1) * 128, :],
                      wait=[("vx1", max(0, S.val("vx1") - 4))], inc=[("lx3", 16)])
                for sl in range(4):
                    def womm(e, sl=sl):
                        last = None
                        for hc in range(16):
                            src = ASL((hc + 16) % 17)
                            last = e.matmul(
                                pwo[sl % 2][:], wob[:, hc * 128:(hc + 1) * 128],
                                src[:, sl * 512:(sl + 1) * 512],
                                start=(hc == 0), stop=(hc == 15))
                        return last
                    S.op("tensor", womm,
                         wait=[("lwo", S.val("lwo")), ("vnr", att_done),
                               ("vx1", max(0, S.val("vx1") - 1))],
                         inc=[("two", 1)], tag=f"womm{oc}_{sl}")
                    S.op("vector", lambda e, sl=sl, bi=bi: e.tensor_add(
                        x1c[:, sl * 512:(sl + 1) * 512], pwo[sl % 2][:],
                        xstg3[bi][:, sl * 512:(sl + 1) * 512]),
                        wait=[("two", S.val("two")), ("lx3", S.val("lx3")),
                              ("gx1", S.val("gx1"))],
                        inc=[("vx1", 1)], tag=f"x1{oc}_{sl}")
                vx1_oc = S.val("vx1")
                S.dma("gpsimd", x1pre_hbm[oc * 128:(oc + 1) * 128, :], x1c[:],
                      wait=[("vx1", vx1_oc)], inc=[("gx1", 16)])
                S.op("vector", lambda e: e.tensor_copy(x1bf[:], x1c[:]),
                     wait=[("vx1", vx1_oc)], inc=[("vx1", 1)])
                S.op("scalar", lambda e: e.activation(sqbf[:], x1bf[:], AF.Square),
                     wait=[("vx1", S.val("vx1"))], inc=[("sx1", 1)])
                for which, (src, ridx) in enumerate([(x1bf, PA), (sqbf, PB)]):
                    for s4 in range(4):
                        S.op("tensor", lambda e, src=src, s4=s4: e.matmul(
                            pst[0:1, :], onesb[:],
                            src[:, s4 * 512:(s4 + 1) * 512], start=True,
                            stop=True),
                            wait=[("sx1", S.val("sx1")), ("vws", S.val("vws"))],
                            inc=[("tst", 1)])
                        S.op("vector", lambda e, ridx=ridx, s4=s4:
                             e.scalar_tensor_tensor(
                                 prows[0:1, ridx * NT + s4 * 512:
                                       ridx * NT + (s4 + 1) * 512],
                                 pst[0:1, :], 1.0,
                                 prows[0:1, ridx * NT + s4 * 512:
                                       ridx * NT + (s4 + 1) * 512],
                                 ALU.mult, ALU.add),
                             wait=[("tst", S.val("tst"))], inc=[("vws", 1)],
                             tag=f"st{oc}_{which}_{s4}")
        wo_done = [("gx1", S.val("gx1")), ("vws", S.val("vws"))]
        S.barrier()
        actx.close()

        if DEBUG and phases == 5:
            with contextlib.ExitStack() as pd:
                xdf = SB(pd, "xdf", [128, NT])
                for dc in range(16):
                    S.dma("sync", xdf[:], x1pre_hbm[dc * 128:(dc + 1) * 128, :],
                          wait=wo_done + [("od", S.val("od"))], inc=[("od", 16)])
                    S.dma("sync", dbg["x1pre"][dc * 128:(dc + 1) * 128, :],
                          xdf[:], wait=[("od", S.val("od"))], inc=[("od", 16)])
                S.dma("sync", dbg["rows"][0:1, :], PR(PA), wait=wo_done,
                      inc=[("od", 16)])
                S.dma("sync", dbg["rows"][1:2, :], PR(PB), inc=[("od", 16)])
                S.dma("sync", out[0:128, :], xq[0:128, :], inc=[("od", 16)])
                S.op("sync", lambda e: e.nop(), wait=[("od", S.val("od"))],
                     inc=[("done", 1)])
            prctx.close()
            with nc.Block() as block:
                S.emit(block)
            return nc

        fctx = contextlib.ExitStack()
        b1bc = SB(fctx, "b1bc", [128, NT], BF16)
        m2r3 = SB(fctx, "m2r3", [128, NT], BF16)

        with contextlib.ExitStack() as p5:
            lrows = SB(p5, "lrows", [1, 6 * NT])
            LR = lambda i: lrows[0:1, i * NT:(i + 1) * NT]
            L_RMS, L_M21, L_M22, L_TA, L_TB, L_K = range(6)
            def ln1rows(e):
                e.tensor_scalar_mul(PR(PA), PR(PA), 1.0 / E)
                e.tensor_scalar_mul(PR(PB), PR(PB), 1.0 / E)
                e.tensor_mul(LR(L_TA), PR(PA), PR(PA))
                e.tensor_sub(PR(PB), PR(PB), LR(L_TA))
                return e.tensor_scalar_add(PR(PB), PR(PB), 1e-5)
            S.op("vector", ln1rows, wait=wo_done, inc=[("vr1", 1)])
            S.op("scalar", lambda e: e.activation(PR(PB), PR(PB), AF.Ln),
                 wait=[("vr1", S.val("vr1"))], inc=[("sr1", 1)])
            S.op("scalar", lambda e: e.activation(PR(PB), PR(PB), AF.Exp,
                                                  scale=-0.5), inc=[("sr1", 1)])
            pbc = PS(p5, "pbc", [128, 512])
            k1bc = SB(p5, "k1bc", [128, NT])
            k2bc = SB(p5, "k2bc", [128, NT])
            with contextlib.ExitStack() as p5a:
                mbc = SB(p5a, "mbc", [128, NT])
                rbc1 = SB(p5a, "rbc1", [128, NT])
                for ridx, dst in [(PA, mbc), (PB, rbc1)]:
                    for s4 in range(4):
                        S.op("tensor", lambda e, ridx=ridx, s4=s4: e.matmul(
                            pbc[:], onesr[:],
                            prows[0:1, ridx * NT + s4 * 512:
                                  ridx * NT + (s4 + 1) * 512],
                            start=True, stop=True),
                            wait=[("sr1", S.val("sr1")), ("vr2", S.val("vr2"))],
                            inc=[("tr1", 1)])
                        S.op("vector", lambda e, dst=dst, s4=s4: e.tensor_copy(
                            dst[:, s4 * 512:(s4 + 1) * 512], pbc[:]),
                            wait=[("tr1", S.val("tr1"))], inc=[("vr2", 1)])
                xstg4 = [SB(p5a, f"xstg4{i}", [128, NT]) for i in range(2)]
                sqb2 = SB(p5a, "sqb2", [128, NT], BF16)
                m2c = SB(p5a, "m2c", [128, NT], BF16)
                m2r1 = SB(p5a, "m2r1", [128, NT], BF16)
                m2r2 = SB(p5a, "m2r2", [128, NT], BF16)
                pst2 = PS(p5a, "pst2", [1, 512])
                S.op("vector", lambda e: e.memset(LR(L_RMS), 0.0), inc=[("vr2", 1)])
                S.op("vector", lambda e: e.memset(m2r1[:], 0.0), inc=[("vr2", 1)])
                S.op("vector", lambda e: e.memset(m2r2[:], 0.0), inc=[("vr2", 1)])
                bc_done = S.val("vr2")
                for dc in range(16):
                    bi = dc % 2
                    S.dma("sync", xstg4[bi][:],
                          x1pre_hbm[dc * 128:(dc + 1) * 128, :],
                          wait=[("vh", max(0, S.val("vh") - 2)),
                                ("sh", max(0, S.val("sh") - 1)),
                                (f"gxh{bi}", S.val(f"gxh{bi}")),
                                ("gx1", S.val("gx1"))], inc=[("lx4", 16)])
                    S.op("vector", lambda e, bi=bi: e.tensor_sub(
                        xstg4[bi][:], xstg4[bi][:], mbc[:]),
                        wait=[("lx4", S.val("lx4")), ("vr2", bc_done),
                              (f"gxh{bi}", S.val(f"gxh{bi}"))],
                        inc=[("vh", 1)])
                    S.op("vector", lambda e, bi=bi: e.tensor_mul(
                        xstg4[bi][:], xstg4[bi][:], rbc1[:]), inc=[("vh", 1)],
                        tag=f"xh{dc}")
                    vh_dc = S.val("vh")
                    S.dma("gpsimd", xhat_hbm[dc * 128:(dc + 1) * 128, :],
                          xstg4[bi][:], wait=[("vh", vh_dc)],
                          inc=[(f"gxh{bi}", 16)])
                    S.op("scalar", lambda e, bi=bi: e.activation(
                        sqb2[:], xstg4[bi][:], AF.Square),
                        wait=[("vh", vh_dc)], inc=[("sh", 1)])
                    for s4 in range(4):
                        S.op("tensor", lambda e, s4=s4: e.matmul(
                            pst2[0:1, :], onesb[:],
                            sqb2[:, s4 * 512:(s4 + 1) * 512], start=True,
                            stop=True),
                            wait=[("sh", S.val("sh")), ("vrm", S.val("vrm"))],
                            inc=[("tst2", 1)])
                        S.op("vector", lambda e, s4=s4: e.scalar_tensor_tensor(
                            lrows[0:1, L_RMS * NT + s4 * 512:
                                  L_RMS * NT + (s4 + 1) * 512],
                            pst2[0:1, :], 1.0,
                            lrows[0:1, L_RMS * NT + s4 * 512:
                                  L_RMS * NT + (s4 + 1) * 512],
                            ALU.mult, ALU.add),
                            wait=[("tst2", S.val("tst2"))], inc=[("vrm", 1)])
                    S.op("vector", lambda e, dc=dc: e.tensor_scalar_mul(
                        m2c[:], sqb2[:], g1sq[:, dc:dc + 1]),
                        wait=[("sh", S.val("sh"))], inc=[("vm2", 1)])
                    S.op("vector", lambda e: e.tensor_max(m2r1[:], m2r1[:],
                                                          m2c[:]),
                         inc=[("vm2", 1)])
                    S.op("vector", lambda e, dc=dc: e.tensor_scalar_mul(
                        m2c[:], sqb2[:], g2sq[:, dc:dc + 1]), inc=[("vm2", 1)])
                    S.op("vector", lambda e: e.tensor_max(m2r2[:], m2r2[:],
                                                          m2c[:]),
                         inc=[("vm2", 1)], tag=f"m2{dc}")
                ptm = PS(p5a, "ptm", [128, 128], BF16)
                mx_tok = SB(p5a, "mx_tok", [128, 16], BF16)
                for which, (m2r, rm2) in enumerate([(m2r1, L_M21),
                                                    (m2r2, L_M22)]):
                    for tcn in range(16):
                        S.op("tensor", lambda e, m2r=m2r, tcn=tcn: e.transpose(
                            ptm[:], m2r[:, tcn * 128:(tcn + 1) * 128], idb[:]),
                            wait=[("vm2", S.val("vm2")), ("vmx", S.val("vmx")),
                                  ("gmx", S.val("gmx"))], inc=[("tmx", 1)])
                        S.op("vector", lambda e, tcn=tcn: e.reduce_max(
                            mx_tok[:, tcn:tcn + 1], ptm[:], axis=AX.X),
                            wait=[("tmx", S.val("tmx"))], inc=[("vmx", 1)])
                    S.dma("gpsimd", m2x_hbm[:], mx_tok[:],
                          wait=[("vmx", S.val("vmx"))], inc=[("gmx", 16)])
                    S.op("gpsimd", lambda e, rm2=rm2: e.dma_start(
                        out=lrows[0:1, rm2 * NT:(rm2 + 1) * NT],
                        in_=m2x_hbm.rearrange("p c -> c p"),
                        allow_slow_non_contiguous=True),
                        wait=[("gmx", S.val("gmx"))], inc=[("gmx", 16)],
                        tag=f"m2row{which}")
            S.op("vector", lambda e: e.tensor_scalar(
                LR(L_RMS), LR(L_RMS), 1.0 / E, 1e-8, ALU.mult, ALU.add),
                wait=[("vrm", S.val("vrm")), ("gmx", S.val("gmx"))],
                inc=[("vr3", 1)])
            S.op("scalar", lambda e: e.activation(LR(L_RMS), LR(L_RMS), AF.Ln),
                 wait=[("vr3", S.val("vr3"))], inc=[("sr2", 1)])
            S.op("scalar", lambda e: e.activation(LR(L_RMS), LR(L_RMS), AF.Exp,
                                                  scale=-0.5), inc=[("sr2", 1)])
            for rm2, ca, kdst, rbeta in [(L_M21, C_A1, k1bc, PA),
                                         (L_M22, C_A2, k2bc, PC)]:
                S.op("scalar", lambda e, rm2=rm2: e.activation(
                    LR(L_TA), LR(rm2), AF.Ln),
                    wait=[("sr2", S.val("sr2")), ("vr4", S.val("vr4"))],
                    inc=[("sr2", 1)])
                S.op("scalar", lambda e: e.activation(
                    LR(L_TA), LR(L_TA), AF.Exp, scale=0.5), inc=[("sr2", 1)])
                def krow_a(e):
                    e.tensor_mul(LR(L_TA), LR(L_TA), LR(L_RMS))
                    e.tensor_scalar_max(LR(L_TA), LR(L_TA), 1e-4)
                    return e.reciprocal(LR(L_TB), LR(L_TA))
                S.op("vector", krow_a, wait=[("sr2", S.val("sr2")),
                                              ("tr2", S.val("tr2"))],
                     inc=[("vr4", 1)])
                def krow_b(e, ca=ca, rbeta=rbeta):
                    e.tensor_scalar_mul(LR(L_K), LR(L_TB), 127.0)
                    e.tensor_scalar_mul(LR(L_TA), LR(L_TA), -1.0 / 127.0)
                    return e.tensor_scalar_mul(PR(rbeta), LR(L_TA), C(ca))
                S.op("vector", krow_b, wait=[("vr4", S.val("vr4"))],
                     inc=[("vr4", 1)], tag=f"krow{rm2}")
                for s4 in range(4):
                    S.op("tensor", lambda e, s4=s4: e.matmul(
                        pbc[:], onesr[:],
                        lrows[0:1, L_K * NT + s4 * 512:L_K * NT + (s4 + 1) * 512],
                        start=True, stop=True),
                        wait=[("vr4", S.val("vr4")), ("vr5", S.val("vr5"))],
                        inc=[("tr2", 1)])
                    S.op("vector", lambda e, kdst=kdst, s4=s4: e.tensor_copy(
                        kdst[:, s4 * 512:(s4 + 1) * 512], pbc[:]),
                        wait=[("tr2", S.val("tr2"))], inc=[("vr5", 1)])
            for s4 in range(4):
                S.op("tensor", lambda e, s4=s4: e.matmul(
                    pbc[:], onesr[:],
                    prows[0:1, PA * NT + s4 * 512:PA * NT + (s4 + 1) * 512],
                    start=True, stop=True),
                    wait=[("vr4", S.val("vr4")), ("vr5", S.val("vr5"))],
                    inc=[("tr2", 1)])
                S.op("vector", lambda e, s4=s4: e.tensor_copy(
                    b1bc[:, s4 * 512:(s4 + 1) * 512], pbc[:]),
                    wait=[("tr2", S.val("tr2"))], inc=[("vr5", 1)])
            kbc_done = S.val("vr5")
            xstg5 = [SB(p5, f"xstg5{i}", [128, NT]) for i in range(2)]
            vq_t = SB(p5, "vq_t", [128, NT])
            yi_t = SB(p5, "yi_t", [128, NT], I32)
            cp_t = SB(p5, "cp_t", [128, NT])
            cm_t = SB(p5, "cm_t", [128, NT])
            ybt = SB(p5, "ybt", [128, NT], BF16)
            for dc in range(16):
                bi = dc % 2
                S.dma("sync", xstg5[bi][:], xhat_hbm[dc * 128:(dc + 1) * 128, :],
                      wait=[("vqz", max(0, S.val("vqz") - 12)),
                            ("gxh0", S.val("gxh0")), ("gxh1", S.val("gxh1"))],
                      inc=[("lx5", 16)])
                for which, (gt, kbct) in enumerate([(g1t, k1bc), (g2t, k2bc)]):
                    S.op("vector", lambda e, bi=bi, dc=dc, gt=gt, kbct=kbct:
                         e.scalar_tensor_tensor(vq_t[:], xstg5[bi][:],
                                                gt[:, dc:dc + 1], kbct[:],
                                                ALU.mult, ALU.mult),
                         wait=[("lx5", S.val("lx5")), ("vr5", kbc_done)],
                         inc=[("vqz", 1)], tag=f"qv{dc}_{which}")
                    S.op("vector", lambda e: e.tensor_copy(yi_t[:], vq_t[:]),
                         inc=[("vqz", 1)])
                    S.op("vector", lambda e: e.scalar_tensor_tensor(
                        cp_t[:], yi_t[:], 0.5, vq_t[:], ALU.add, ALU.is_lt),
                        inc=[("vqz", 1)])
                    S.op("vector", lambda e: e.scalar_tensor_tensor(
                        cm_t[:], yi_t[:], -0.5, vq_t[:], ALU.add, ALU.is_gt),
                        inc=[("vqz", 1)])
                    S.op("vector", lambda e: e.scalar_tensor_tensor(
                        cp_t[:], cm_t[:], -1.0, cp_t[:], ALU.mult, ALU.add),
                        inc=[("vqz", 1)])
                    S.op("vector", lambda e: e.scalar_tensor_tensor(
                        ybt[:], yi_t[:], 1.0, cp_t[:], ALU.mult, ALU.add),
                        wait=[("gy", S.val("gy"))],
                        inc=[("vqz", 1)], tag=f"qy{dc}_{which}")
                    S.dma("gpsimd", y_hbm[which, dc * 128:(dc + 1) * 128, :],
                          ybt[:], wait=[("vqz", S.val("vqz"))], inc=[("gy", 16)])
        p5_done = [("gy", S.val("gy"))]
        S.barrier()

        if DEBUG and phases == 6:
            with contextlib.ExitStack() as pd:
                ydf = SB(pd, "ydf", [128, NT], BF16)
                ydf2 = SB(pd, "ydf2", [128, NT])
                for which, dk in [(0, "y1"), (1, "y2")]:
                    for dc in range(16):
                        S.dma("sync", ydf[:],
                              y_hbm[which, dc * 128:(dc + 1) * 128, :],
                              wait=p5_done + [("vd", S.val("vd"))],
                              inc=[("od", 16)])
                        S.op("vector", lambda e: e.tensor_copy(ydf2[:], ydf[:]),
                             wait=[("od", S.val("od"))], inc=[("vd", 1)])
                        S.dma("sync", dbg[dk][dc * 128:(dc + 1) * 128, :],
                              ydf2[:], wait=[("vd", S.val("vd"))],
                              inc=[("od", 16)])
                S.dma("sync", dbg["rows"][2:3, :], PR(PA), wait=p5_done,
                      inc=[("od", 16)])
                S.dma("sync", dbg["rows"][3:4, :], PR(PC), inc=[("od", 16)])
                S.dma("sync", dbg["rows"][4:5, :], k1bc[0:1, :], inc=[("od", 16)])
                S.dma("sync", dbg["rows"][5:6, :], k2bc[0:1, :], inc=[("od", 16)])
                for dc in range(16):
                    S.dma("sync", dbg["xhat"][dc * 128:(dc + 1) * 128, :],
                          xhat_hbm[dc * 128:(dc + 1) * 128, :],
                          wait=[("od", S.val("od"))], inc=[("od", 16)])
                S.dma("sync", dbg["srow"][0:1, 0:4], cells[0:1, 0:4],
                      inc=[("od", 16)])
                S.dma("sync", dbg["srow"][1:2, 0:4], cells[0:1, 4:8],
                      inc=[("od", 16)])
                S.dma("sync", dbg["srow"][2:3, 0:4], csnap[0:1, 3:7],
                      inc=[("od", 16)])
                S.dma("sync", dbg["srow"][3:4, 0:4], csnap[0:1, 19:23],
                      inc=[("od", 16)])
                S.dma("sync", out[0:128, :], xq[0:128, :], inc=[("od", 16)])
                S.op("sync", lambda e: e.nop(), wait=[("od", S.val("od"))],
                     inc=[("done", 1)])
            fctx.close(); prctx.close()
            with nc.Block() as block:
                S.emit(block)
            return nc

        with contextlib.ExitStack() as p6:
            y1b = SB(p6, "y1b", [128, 16 * NT], BF16)
            y2b = SB(p6, "y2b", [128, 16 * NT], BF16)
            for which, yb in [(0, y1b), (1, y2b)]:
                for dc in range(16):
                    S.dma("sync", yb[:, dc * NT:(dc + 1) * NT],
                          y_hbm[which, dc * 128:(dc + 1) * 128, :],
                          wait=p5_done, inc=[("ly", 16)])
            y_ld = S.val("ly")
            w1s = SB(p6, "w1s", [128, 2048])
            w2s = SB(p6, "w2s", [128, 2048])
            t1b = [SB(p6, f"t1b{i}", [128, 2048], BF16) for i in range(2)]
            t2b = [SB(p6, f"t2b{i}", [128, 2048], BF16) for i in range(2)]
            tf = SB(p6, "tf", [128, 512])
            psib = [SB(p6, f"psib{i}", [128, 512], BF16) for i in range(2)]
            sqb3 = SB(p6, "sqb3", [128, 512], BF16)
            pu1 = [PS(p6, f"pu1{i}", [128, 512]) for i in range(2)]
            pu2 = [PS(p6, f"pu2{i}", [128, 512]) for i in range(2)]
            pst3 = [PS(p6, f"pst3{i}", [1, 512]) for i in range(4)]
            S.op("vector", lambda e: e.memset(m2r3[:], 0.0), inc=[("vb1", 1)])
            b1_done = S.val("vb1")
            tu0 = S.val("tu")
            tss0 = S.val("tss")
            vm30 = S.val("vm3")
            ssq_after = {}
            pend_stat = None
            def emit_stat():
                nonlocal pend_stat
                if pend_stat is None:
                    return
                gsl, sl, par = pend_stat
                hcp = gsl // 4
                S.op("tensor", lambda e, sl=sl, hcp=hcp: e.matmul(
                    pst3[sl][0:1, :], onesb[:], sqb3[:],
                    start=(hcp == 0), stop=(hcp == NHC - 1)),
                    wait=[("ssq", ssq_after[gsl])], inc=[("tss", 1)],
                    tag=f"st3_{gsl}")
                pend_stat = None
            for hc in range(NHC):
                tp = hc % 2
                S.dma("sync", w1s[:], w1t[hc],
                      wait=[("vt12", S.val("vt12"))], inc=[("lw1", 16)])
                S.dma("sync", w2s[:], w2t[hc], inc=[("lw1", 16)])
                tern_wait = [("lw1", S.val("lw1"))]
                if hc >= 2:
                    tern_wait.append(("tu", tu0 + 8 * (hc - 1)))
                if hc == 0:
                    tern_wait += p2_done
                S.op("vector", lambda e, tp=tp: e.tensor_scalar(
                    t1b[tp][:], w1s[:], thcol[:, 0:1], None, ALU.is_ge),
                    wait=tern_wait, inc=[("vt12", 1)])
                S.op("vector", lambda e, tp=tp: e.scalar_tensor_tensor(
                    t1b[tp][:], w1s[:], thcol[:, 1:2], t1b[tp][:], ALU.is_lt,
                    ALU.subtract), inc=[("vt12", 1)])
                S.op("vector", lambda e, tp=tp: e.tensor_scalar(
                    t2b[tp][:], w2s[:], thcol[:, 2:3], None, ALU.is_ge),
                    inc=[("vt12", 1)])
                S.op("vector", lambda e, tp=tp: e.scalar_tensor_tensor(
                    t2b[tp][:], w2s[:], thcol[:, 3:4], t2b[tp][:], ALU.is_lt,
                    ALU.subtract), inc=[("vt12", 1)], tag=f"tern{hc}")
                vt_hc = S.val("vt12")
                for sl in range(4):
                    gsl = hc * 4 + sl
                    sbi = sl % 2
                    par = gsl % 2
                    def u1mm(e, sbi=sbi, sl=sl, tp=tp):
                        last = None
                        for dc in range(16):
                            last = e.matmul(
                                pu1[sbi][:], t1b[tp][:, dc * 128:(dc + 1) * 128],
                                y1b[:, dc * NT + sl * 512:dc * NT + (sl + 1) * 512],
                                start=(dc == 0), stop=(dc == 15))
                        return last
                    S.op("tensor", u1mm,
                         wait=[("vt12", vt_hc), ("ly", y_ld),
                               ("vu1", max(0, S.val("vu1") - 1))],
                         inc=[("tu", 1)], tag=f"u1mm{hc}_{sl}")
                    def u2mm(e, sbi=sbi, sl=sl, tp=tp):
                        last = None
                        for dc in range(16):
                            last = e.matmul(
                                pu2[sbi][:], t2b[tp][:, dc * 128:(dc + 1) * 128],
                                y2b[:, dc * NT + sl * 512:dc * NT + (sl + 1) * 512],
                                start=(dc == 0), stop=(dc == 15))
                        return last
                    S.op("tensor", u2mm,
                         wait=[("vps", max(0, S.val("vps") - 1))],
                         inc=[("tu", 1)], tag=f"u2mm{hc}_{sl}")
                    emit_stat()
                    S.op("vector", lambda e, sbi=sbi, sl=sl: e.tensor_mul(
                        tf[:], pu1[sbi][:], b1bc[:, sl * 512:(sl + 1) * 512]),
                        wait=[("tu", tu0 + 2 * gsl + 1), ("vps", S.val("vps"))],
                        inc=[("vu1", 1)], tag=f"bt{hc}_{sl}")
                    S.op("scalar", lambda e: e.activation(tf[:], tf[:], AF.Silu),
                         wait=[("vu1", S.val("vu1"))], inc=[("ssw", 1)])
                    psi_wait = [("ssw", S.val("ssw")), ("tu", tu0 + 2 * gsl + 2),
                                (f"gps{par}", S.val(f"gps{par}"))]
                    if gsl >= 2:
                        psi_wait.append(("ssq", ssq_after[gsl - 2]))
                    S.op("vector", lambda e, sbi=sbi, par=par: e.tensor_mul(
                        psib[par][:], tf[:], pu2[sbi][:]),
                        wait=psi_wait, inc=[("vps", 1)], tag=f"psi{hc}_{sl}")
                    vps_sl = S.val("vps")
                    S.dma("gpsimd", psi_hbm[sl, hc * 128:(hc + 1) * 128, :],
                          psib[par][:],
                          wait=[("vps", vps_sl)], inc=[(f"gps{par}", 16)])
                    sq_wait = [("vps", vps_sl)]
                    if gsl >= 1:
                        sq_wait += [("tss", tss0 + gsl), ("vm3", vm30 + gsl)]
                    S.op("scalar", lambda e, par=par: e.activation(
                        sqb3[:], psib[par][:],
                        AF.Square), wait=sq_wait, inc=[("ssq", 1)])
                    ssq_after[gsl] = S.val("ssq")
                    pend_stat = (gsl, sl, par)
                    m2_wait = [("ssq", S.val("ssq"))]
                    if gsl == 0:
                        m2_wait.append(("vb1", b1_done))
                    S.op("vector", lambda e, hc=hc, sl=sl:
                         e.scalar_tensor_tensor(
                             m2r3[:, sl * 512:(sl + 1) * 512], sqb3[:],
                             g3sq[:, hc:hc + 1],
                             m2r3[:, sl * 512:(sl + 1) * 512],
                             ALU.mult, ALU.max),
                         wait=m2_wait, inc=[("vm3", 1)], tag=f"m3{hc}_{sl}")
            emit_stat()
            tss_all = S.val("tss")
            for sl in range(4):
                S.op("vector", lambda e, sl=sl: e.tensor_copy(
                    prows[0:1, PB * NT + sl * 512:PB * NT + (sl + 1) * 512],
                    pst3[sl][0:1, :]),
                    wait=[("tss", tss_all)], inc=[("vsr", 1)])
        p6_done = [(f"gps{i}", S.val(f"gps{i}")) for i in range(2)] + [
            ("vsr", S.val("vsr")), ("vm3", S.val("vm3"))]
        S.barrier()

        if DEBUG and phases == 7:
            with contextlib.ExitStack() as pd:
                S.dma("sync", dbg["rows"][0:1, :], PR(PB),
                      wait=p6_done + [("od", S.val("od"))], inc=[("od", 16)])
                mdf = SB(pd, "mdf", [128, NT])
                S.op("vector", lambda e: e.tensor_copy(mdf[:], m2r3[:]),
                     wait=p6_done, inc=[("vd", 1)])
                S.dma("sync", dbg["x2"][0:128, :], mdf[:],
                      wait=[("vd", S.val("vd"))], inc=[("od", 16)])
                ydf7 = SB(pd, "ydf7", [128, 512], BF16)
                ydf7f = SB(pd, "ydf7f", [128, 512])
                for sl4 in range(4):
                    S.dma("sync", ydf7[:],
                          psi_hbm[sl4, 0:128, :],
                          wait=[("vd", S.val("vd"))], inc=[("od", 16)])
                    S.op("vector", lambda e, sl4=sl4: e.tensor_copy(
                        ydf7f[:], ydf7[:]),
                        wait=[("od", S.val("od"))], inc=[("vd", 1)])
                    S.dma("sync", dbg["x2"][128:256, sl4 * 512:(sl4 + 1) * 512],
                          ydf7f[:], wait=[("vd", S.val("vd"))], inc=[("od", 16)])
                S.dma("sync", out[0:128, :], xq[0:128, :], inc=[("od", 16)])
                S.op("sync", lambda e: e.nop(), wait=[("od", S.val("od"))],
                     inc=[("done", 1)])
            fctx.close(); prctx.close()
            with nc.Block() as block:
                S.emit(block)
            return nc

        with contextlib.ExitStack() as p7:
            lr7 = SB(p7, "lr7", [1, 3 * NT])
            L7 = lambda i: lr7[0:1, i * NT:(i + 1) * NT]
            def hrows1(e):
                e.tensor_mul(L7(0), PR(PC), PR(PC))
                e.tensor_mul(L7(1), PR(PB), L7(0))
                return e.tensor_scalar(L7(1), L7(1), 1.0 / H, 1e-8, ALU.mult,
                                       ALU.add)
            S.op("vector", hrows1, wait=p6_done, inc=[("vr7", 1)])
            S.op("scalar", lambda e: e.activation(L7(1), L7(1), AF.Ln),
                 wait=[("vr7", S.val("vr7"))], inc=[("sr7", 1)])
            S.op("scalar", lambda e: e.activation(L7(1), L7(1), AF.Exp,
                                                  scale=-0.5), inc=[("sr7", 1)])
            ptm3 = PS(p7, "ptm3", [128, 128], BF16)
            mx_tok3 = SB(p7, "mx_tok3", [128, 16], BF16)
            for tcn in range(16):
                S.op("tensor", lambda e, tcn=tcn: e.transpose(
                    ptm3[:], m2r3[:, tcn * 128:(tcn + 1) * 128], idb[:]),
                    wait=p6_done + [("vx7", S.val("vx7"))], inc=[("tx7", 1)])
                S.op("vector", lambda e, tcn=tcn: e.reduce_max(
                    mx_tok3[:, tcn:tcn + 1], ptm3[:], axis=AX.X),
                    wait=[("tx7", S.val("tx7"))], inc=[("vx7", 1)])
            S.dma("gpsimd", m2x_hbm[:], mx_tok3[:],
                  wait=[("vx7", S.val("vx7"))], inc=[("gx7", 16)])
            S.op("gpsimd", lambda e: e.dma_start(
                out=L7(2), in_=m2x_hbm.rearrange("p c -> c p"),
                allow_slow_non_contiguous=True),
                wait=[("gx7", S.val("gx7"))], inc=[("gx7", 16)])
            S.op("scalar", lambda e: e.activation(L7(2), L7(2), AF.Ln),
                 wait=[("gx7", S.val("gx7")), ("sr7", S.val("sr7"))],
                 inc=[("sr7", 1)])
            S.op("scalar", lambda e: e.activation(L7(2), L7(2), AF.Exp,
                                                  scale=0.5), inc=[("sr7", 1)])
            def hrows2a(e):
                e.tensor_scalar_mul(L7(0), PR(PC), -1.0)
                e.tensor_mul(L7(2), L7(2), L7(0))
                e.tensor_mul(L7(2), L7(2), L7(1))
                e.tensor_scalar_max(L7(2), L7(2), 1e-4)
                e.tensor_mul(L7(0), L7(0), L7(1))
                return e.reciprocal(PR(PD), L7(2))
            S.op("vector", hrows2a, wait=[("sr7", S.val("sr7"))], inc=[("vr7", 1)])
            def hrows2b(e):
                e.tensor_scalar_mul(PR(PD), PR(PD), 127.0)
                e.tensor_mul(L7(0), L7(0), PR(PD))
                e.tensor_scalar_mul(PR(PC), L7(0), -1.0)
                e.tensor_scalar_mul(L7(2), L7(2), -1.0 / 127.0)
                return e.tensor_scalar_mul(PR(PD), L7(2), C(C_A3))
            S.op("vector", hrows2b, wait=[("vr7", S.val("vr7"))], inc=[("vr7", 1)])
        p7_done = S.val("vr7")
        S.barrier()
        fctx.close()

        with contextlib.ExitStack() as p8:
            k3bc = SB(p8, "k3bc", [128, NT]); b3bc = SB(p8, "b3bc", [128, NT])
            pbc3 = PS(p8, "pbc3", [128, 512])
            for ridx, dst in [(PC, k3bc), (PD, b3bc)]:
                for s4 in range(4):
                    S.op("tensor", lambda e, ridx=ridx, s4=s4: e.matmul(
                        pbc3[:], onesr[:],
                        prows[0:1, ridx * NT + s4 * 512:ridx * NT + (s4 + 1) * 512],
                        start=True, stop=True),
                        wait=[("vr7", p7_done), ("vb3", S.val("vb3"))],
                        inc=[("tb3", 1)])
                    S.op("vector", lambda e, dst=dst, s4=s4: e.tensor_copy(
                        dst[:, s4 * 512:(s4 + 1) * 512], pbc3[:]),
                        wait=[("tb3", S.val("tb3"))], inc=[("vb3", 1)])
            b3_done = S.val("vb3")
            S.op("vector", lambda e: e.memset(PR(PA), 0.0), inc=[("vb3", 1)])
            S.op("vector", lambda e: e.memset(PR(PB), 0.0), inc=[("vb3", 1)])
            pa_init = S.val("vb3")
            t80 = S.val("t8")
            v8x0 = S.val("v8x")
            s80 = S.val("s8")
            t8s0 = S.val("t8s")
            v8s0 = S.val("v8s")
            gx20 = S.val("gx2")
            l8x0 = S.val("l8x")
            y3T = [SB(p8, f"y3T{i}", [128, NHC * 512], BF16) for i in range(2)]
            psg = [SB(p8, f"psg{i}", [128, 512], BF16) for i in range(2)]
            t3s = SB(p8, "t3s", [128, 8192], BF16)
            xh8 = SB(p8, "xh8", [128, 512])
            vq8 = SB(p8, "vq8", [128, 512])
            vq9 = vq8
            yi8 = SB(p8, "yi8", [128, 512], I32)
            cp8 = SB(p8, "cp8", [128, 512], BF16)
            cm8 = SB(p8, "cm8", [128, 512], BF16)
            sq8 = SB(p8, "sq8", [128, 512], BF16)
            x2w = [SB(p8, f"x2w{i}", [128, 512], BF16) for i in range(2)]
            pw3 = [PS(p8, f"pw3{i}", [128, 512]) for i in range(2)]
            pst8 = [PS(p8, f"pst8{i}", [1, 512]) for i in range(2)]

            def emit_quant(T, hc):
                bi = hc % 2
                Tp = T % 2
                S.dma("gpsimd", psg[bi][:],
                      psi_hbm[T, hc * 128:(hc + 1) * 128, :],
                      wait=[("v8q", max(0, S.val("v8q") - 11))],
                      inc=[("g8", 16)])
                S.op("vector", lambda e, bi=bi, hc=hc, T=T:
                     e.scalar_tensor_tensor(
                         vq8[:], psg[bi][:], g3t[:, hc:hc + 1],
                         k3bc[:, T * 512:(T + 1) * 512], ALU.mult, ALU.mult),
                     wait=[("g8", S.val("g8")), ("vb3", b3_done)],
                     inc=[("v8q", 1)], tag=f"q3v{T}_{hc}")
                S.op("vector", lambda e: e.tensor_copy(yi8[:], vq8[:]),
                     inc=[("v8q", 1)])
                S.op("vector", lambda e: e.scalar_tensor_tensor(
                    cp8[:], yi8[:], 0.5, vq8[:], ALU.add, ALU.is_lt),
                    inc=[("v8q", 1)])
                S.op("vector", lambda e: e.scalar_tensor_tensor(
                    cm8[:], yi8[:], -0.5, vq8[:], ALU.add, ALU.is_gt),
                    inc=[("v8q", 1)])
                S.op("vector", lambda e: e.scalar_tensor_tensor(
                    cp8[:], cm8[:], -1.0, cp8[:], ALU.mult, ALU.add),
                    inc=[("v8q", 1)])
                qy_wait = [("t8", t80 + 32 * (T - 1))] if T >= 2 else []
                S.op("vector", lambda e, hc=hc, Tp=Tp: e.scalar_tensor_tensor(
                    y3T[Tp][:, hc * 512:(hc + 1) * 512], yi8[:], 1.0, cp8[:],
                    ALU.mult, ALU.add),
                    wait=qy_wait, inc=[("v8q", 1)], tag=f"q3y{T}_{hc}")

            for hc in range(NHC):
                emit_quant(0, hc)
            v8q_after_T = {0: S.val("v8q")}
            pend8 = None

            def emit_stat8(T):
                nonlocal pend8
                if pend8 is None:
                    return
                gm, oc = pend8, pend8 % 16
                sa_wait = [("v8x", v8x0 + 2 * gm + 2)]
                sb_wait = [("s8", s80 + gm + 1)]
                if oc == 0 and T > 0:
                    sa_wait.append(("v8s", v8s0 + 2 * T))
                    sb_wait.append(("v8s", v8s0 + 2 * T))
                S.op("tensor", lambda e, gm=gm, oc=oc: e.matmul(
                    pst8[0][0:1, :], onesb[:], x2w[gm % 2][:],
                    start=(oc == 0), stop=(oc == 15)),
                    wait=sa_wait, inc=[("t8s", 1)], tag=f"stA{gm}")
                S.op("tensor", lambda e, oc=oc: e.matmul(
                    pst8[1][0:1, :], onesb[:], sq8[:],
                    start=(oc == 0), stop=(oc == 15)),
                    wait=sb_wait, inc=[("t8s", 1)], tag=f"stB{gm}")
                pend8 = None

            for T in range(4):
                for oc in range(16):
                    gm = T * 16 + oc
                    obi = oc % 2
                    Tp = T % 2
                    lo_wait = [("t8", t80 + 2 * gm - 1)] if gm > 0 else []
                    hi_wait = [("t8", t80 + 2 * gm)] if gm > 0 else []
                    S.dma("sync", t3s[:, 0:4096], t3_hbm[oc, :, 0:4096],
                          wait=lo_wait, inc=[("l83", 16)])
                    l83_lo = S.val("l83")
                    S.dma("sync", t3s[:, 4096:8192], t3_hbm[oc, :, 4096:8192],
                          wait=hi_wait, inc=[("l83", 16)])
                    l83_hi = S.val("l83")

                    def mmlo(e, obi=obi, Tp=Tp):
                        last = None
                        for hcc in range(32):
                            last = e.matmul(
                                pw3[obi][:], t3s[:, hcc * 128:(hcc + 1) * 128],
                                y3T[Tp][:, hcc * 512:(hcc + 1) * 512],
                                start=(hcc == 0), stop=False)
                        return last
                    S.op("tensor", mmlo,
                         wait=[("l83", l83_lo), ("v8q", v8q_after_T[T])] +
                              ([("v8x", v8x0 + 2 * gm - 3)] if gm >= 2 else []),
                         inc=[("t8", 1)], tag=f"w3lo{T}_{oc}")

                    def mmhi(e, obi=obi, Tp=Tp):
                        last = None
                        for hcc in range(32, 64):
                            last = e.matmul(
                                pw3[obi][:], t3s[:, hcc * 128:(hcc + 1) * 128],
                                y3T[Tp][:, hcc * 512:(hcc + 1) * 512],
                                start=False, stop=(hcc == 63))
                        return last
                    S.op("tensor", mmhi, wait=[("l83", l83_hi)],
                         inc=[("t8", 1)], tag=f"w3hi{T}_{oc}")
                    emit_stat8(T)

                    S.dma("sync", xh8[:],
                          xhat_hbm[oc * 128:(oc + 1) * 128,
                                   T * 512:(T + 1) * 512],
                          wait=([("v8x", v8x0 + 2 * gm)] if gm > 0 else []),
                          inc=[("l8x", 16)])
                    S.op("vector", lambda e, obi=obi, T=T: e.tensor_mul(
                        vq9[:], pw3[obi][:], b3bc[:, T * 512:(T + 1) * 512]),
                        wait=[("t8", t80 + 2 * gm + 2),
                              ("l8x", l8x0 + 16 * (gm + 1))],
                        inc=[("v8x", 1)], tag=f"bx{T}_{oc}")
                    add_wait = ([("s8", s80 + gm - 1),
                                 ("gx2", gx20 + 16 * (gm - 1))] if gm >= 1
                                else []) + \
                               ([("t8s", t8s0 + 2 * gm - 3)] if gm >= 2 else [])
                    S.op("vector", lambda e, obi=obi: e.tensor_add(
                        x2w[obi][:], vq9[:], xh8[:]),
                        wait=add_wait, inc=[("v8x", 1)], tag=f"x2{T}_{oc}")
                    S.dma("gpsimd",
                          x2_hbm[oc * 128:(oc + 1) * 128, T * 512:(T + 1) * 512],
                          x2w[obi][:], wait=[("v8x", v8x0 + 2 * gm + 2)],
                          inc=[("gx2", 16)])
                    sq_wait8 = [("v8x", v8x0 + 2 * gm + 2)]
                    if gm >= 1:
                        sq_wait8.append(("t8s", t8s0 + 2 * gm))
                    S.op("scalar", lambda e, obi=obi: e.activation(
                        sq8[:], x2w[obi][:], AF.Square),
                        wait=sq_wait8, inc=[("s8", 1)])
                    pend8 = gm
                    if T < 3:
                        for hh in range(4):
                            emit_quant(T + 1, 4 * oc + hh)
                emit_stat8(T)
                if T < 3:
                    v8q_after_T[T + 1] = S.val("v8q")
                ra_wait = [("t8s", t8s0 + 32 * (T + 1))]
                if T == 0:
                    ra_wait.append(("vb3", pa_init))
                for which, ridx in [(0, PA), (1, PB)]:
                    S.op("vector", lambda e, which=which, ridx=ridx, T=T:
                         e.scalar_tensor_tensor(
                             prows[0:1, ridx * NT + T * 512:
                                   ridx * NT + (T + 1) * 512],
                             pst8[which][0:1, :], 1.0,
                             prows[0:1, ridx * NT + T * 512:
                                   ridx * NT + (T + 1) * 512],
                             ALU.mult, ALU.add),
                         wait=ra_wait, inc=[("v8s", 1)], tag=f"l2s{T}_{which}")
        p8_done = [("v8s", S.val("v8s")), ("v8x", S.val("v8x")),
                   ("gx2", S.val("gx2"))]
        S.barrier()

        if DEBUG and phases == 8:
            with contextlib.ExitStack() as pd:
                S.dma("sync", dbg["rows"][0:1, :], PR(PA),
                      wait=p8_done + [("od", S.val("od"))], inc=[("od", 16)])
                S.dma("sync", dbg["rows"][1:2, :], PR(PB), inc=[("od", 16)])
                xdf8 = SB(pd, "xdf8", [128, NT], BF16)
                xdf8f = SB(pd, "xdf8f", [128, NT])
                for dc in range(16):
                    S.dma("sync", xdf8[:], x2_hbm[dc * 128:(dc + 1) * 128, :],
                          wait=[("vd", S.val("vd"))], inc=[("od", 16)])
                    S.op("vector", lambda e: e.tensor_copy(xdf8f[:], xdf8[:]),
                         wait=[("od", S.val("od"))], inc=[("vd", 1)])
                    S.dma("sync", dbg["x2"][dc * 128:(dc + 1) * 128, :],
                          xdf8f[:], wait=[("vd", S.val("vd"))], inc=[("od", 16)])
                S.dma("sync", out[0:128, :], xq[0:128, :], inc=[("od", 16)])
                S.op("sync", lambda e: e.nop(), wait=[("od", S.val("od"))],
                     inc=[("done", 1)])
            prctx.close()
            with nc.Block() as block:
                S.emit(block)
            return nc

        with contextlib.ExitStack() as p9:
            mbc2 = SB(p9, "mbc2", [128, NT]); rbc2 = SB(p9, "rbc2", [128, NT])
            pbc4 = PS(p9, "pbc4", [128, 512])
            def ln2rows(e):
                e.tensor_scalar_mul(PR(PA), PR(PA), 1.0 / E)
                e.tensor_scalar_mul(PR(PB), PR(PB), 1.0 / E)
                e.tensor_mul(PR(PC), PR(PA), PR(PA))
                e.tensor_sub(PR(PB), PR(PB), PR(PC))
                return e.tensor_scalar_add(PR(PB), PR(PB), 1e-5)
            S.op("vector", ln2rows, wait=p8_done, inc=[("vr9", 1)])
            S.op("scalar", lambda e: e.activation(PR(PB), PR(PB), AF.Ln),
                 wait=[("vr9", S.val("vr9"))], inc=[("sr9", 1)])
            S.op("scalar", lambda e: e.activation(PR(PB), PR(PB), AF.Exp,
                                                  scale=-0.5), inc=[("sr9", 1)])
            for ridx, dst in [(PA, mbc2), (PB, rbc2)]:
                for s4 in range(4):
                    S.op("tensor", lambda e, ridx=ridx, s4=s4: e.matmul(
                        pbc4[:], onesr[:],
                        prows[0:1, ridx * NT + s4 * 512:ridx * NT + (s4 + 1) * 512],
                        start=True, stop=True),
                        wait=[("sr9", S.val("sr9")), ("vr10", S.val("vr10"))],
                        inc=[("tr9", 1)])
                    S.op("vector", lambda e, dst=dst, s4=s4: e.tensor_copy(
                        dst[:, s4 * 512:(s4 + 1) * 512], pbc4[:]),
                        wait=[("tr9", S.val("tr9"))], inc=[("vr10", 1)])
            bc9 = S.val("vr10")
            of = [SB(p9, f"of{i}", [128, NT]) for i in range(2)]
            xf9 = [SB(p9, f"xf9{i}", [128, NT], BF16) for i in range(2)]
            for oc in range(16):
                bi = oc % 2
                S.dma("sync", xf9[bi][:], x2_hbm[oc * 128:(oc + 1) * 128, :],
                      wait=[("vo", max(0, S.val("vo") - 3))], inc=[("lx9", 16)])
                S.op("vector", lambda e, bi=bi: e.tensor_sub(
                    of[bi][:], xf9[bi][:], mbc2[:]),
                    wait=[("vr10", bc9), ("lx9", S.val("lx9")),
                          (f"go{bi}", S.val(f"go{bi}"))],
                    inc=[("vo", 1)])
                S.op("vector", lambda e, bi=bi: e.tensor_mul(
                    of[bi][:], of[bi][:], rbc2[:]), inc=[("vo", 1)],
                    tag=f"out{oc}")
                S.dma("sync", out[oc * 128:(oc + 1) * 128, :], of[bi][:],
                      wait=[("vo", S.val("vo"))], inc=[(f"go{bi}", 16)])
            S.op("sync", lambda e: e.nop(),
                 wait=[("go0", S.val("go0")), ("go1", S.val("go1"))],
                 inc=[("done", 1)])
        prctx.close()

        with nc.Block() as block:
            S.emit(block)
    return nc


_CACHE = {}


def kernel(**inputs):
    in_maps = prep_inputs(inputs)
    if "nc" not in _CACHE:
        _CACHE["nc"] = build()
    res = run_bass_kernel_spmd(_CACHE["nc"], in_maps, core_ids=list(range(8)))
    outp = np.empty((B, L, E), dtype=np.float32)
    for c in range(8):
        b, par = c >> 1, c & 1
        qcols = np.concatenate([np.arange(g * 512, (g + 1) * 512)
                                for g in Q_TILES[par]])
        outp[b, qcols, :] = np.asarray(res.results[c]["out"]).T
    return outp


# version 07 (final)
# speedup vs baseline: 1.0466x; 1.0466x over previous
import contextlib
import os
import numpy as np
import ml_dtypes
import concourse.bass as bass
import concourse.mybir as mybir
from concourse.bass_utils import run_bass_kernel_spmd

F32 = mybir.dt.float32
BF16 = mybir.dt.bfloat16
I32 = mybir.dt.int32
AF = mybir.ActivationFunctionType
ALU = mybir.AluOpType
AX = mybir.AxisListType
bfdt = ml_dtypes.bfloat16

B, L, E = 4, 4096, 2048
NQH, NKV, DH = 16, 2, 128
H = 8192
NT = 2048
NDC = E // 128
NHC = H // 128
NKC = L // 128
CAPS = [8, 16, 24, 32]
SM_SCALE = 1.0 / float(np.sqrt(np.float64(DH)))
EXP_BIAS = -12.0
ATANH_HALF = float(np.arctanh(0.5))
Q_TILES = {0: [0, 3, 4, 7], 1: [1, 2, 5, 6]}
DEBUG = os.environ.get("ANT_KDBG", "")


class Sched:
    ENGS = ("sync", "vector", "scalar", "tensor", "gpsimd")

    def __init__(self, nc):
        self.nc = nc
        self.items = {e: [] for e in self.ENGS}
        self.sems = {}
        self.cnt = {}

    ALIAS = {"vxq": "v1", "lxq": "g5", "lq": "g6", "tq": "t1", "sq": "s1", "vq2": "v2", "gq": "g1", "lw": "g5", "lx": "g6", "tkv": "t1", "skv": "s1", "vkr": "v2", "gkd": "g1", "vtr": "v3", "ttr": "t2", "vab": "v1", "lab": "l1", "tab": "t1", "vab2": "v2", "vab3": "v4", "sab": "s1", "vt3": "v5", "lt3": "l1", "gt3": "g7", "vmk": "v1", "se": "s1", "tsc": "t1", "tav": "t2", "ve": "v2", "vnr": "v3", "trb": "t3", "tsync": "t4", "lwo": "g6", "two": "t1", "vx1": "v1", "lx3": "l1", "gx1": "g1", "sx1": "s1", "tst": "t2", "vws": "v2", "vr1": "v1", "sr1": "s1", "tr1": "t1", "vr2": "v2", "vh": "v3", "lx4": "l1", "gxh0": "g2", "gxh1": "g3", "sh": "s2", "tst2": "t2", "vrm": "v4", "vm2": "v5", "tmx": "t3", "vmx": "v6", "gmx": "g4", "vr3": "v7", "sr2": "s3", "vr4": "v8", "tr2": "t4", "vr5": "v9", "vqz": "v10", "lx5": "l2", "gy": "g1", "ly": "l1", "vt12": "v1", "lw1": "l2", "tu": "t1", "vu1": "v2", "vps": "v3", "ssw": "s1", "gps0": "g1", "gps1": "g2", "gps2": "g3", "gps3": "g4", "ssq": "s2", "tss": "t2", "vsr": "v4", "vm3": "v5", "vr7": "v1", "sr7": "s1", "vx7": "v2", "tx7": "t1", "gx7": "g1", "vb3": "v1", "tb3": "t1", "v8q": "v2", "l8": "l1", "t8": "t2", "l83": "l2", "v8x": "v3", "l8x": "l3", "s8": "s1", "t8s": "t3", "v8s": "v4", "vr9": "v1", "sr9": "s1", "tr9": "t1", "vr10": "v2", "vo": "v3", "go0": "l4", "go1": "l5", "gid": "g4", "gx2": "g5", "lx9": "l1", "g8": "g6"}

    def sem(self, name):
        name = self.ALIAS.get(name, name)
        if name not in self.sems:
            self.sems[name] = self.nc.alloc_semaphore(name=name)
            self.cnt[name] = 0
        return self.sems[name]

    def val(self, name):
        name = self.ALIAS.get(name, name)
        self.sem(name)
        return self.cnt[name]

    def op(self, eng, fn, wait=(), inc=(), tag=""):
        wait = [(self.ALIAS.get(s, s), v) for s, v in wait if v > 0]
        inc = [(self.ALIAS.get(s, s), n) for s, n in inc]
        if getattr(self, "epoch", 0) > 0:
            wait = [("ph", self.epoch)] + wait
        for s, _ in wait:
            self.sem(s)
        for s, n in inc:
            self.sem(s)
            self.cnt[s] += n
        self.items[eng].append((fn, wait, list(inc), tag))

    def barrier(self):
        waits = [(n, v) for n, v in self.cnt.items()
                 if v > 0 and n not in ("done", "ph")]
        self.epoch = getattr(self, "epoch", 0)
        for eng in self.ENGS:
            self.op(eng, lambda e: e.nop(), wait=waits, inc=[("ph", 1)],
                    tag=f"barrier{self.epoch}")
        self.epoch += 5

    def dma(self, eng, out, in_, wait=(), inc=(), tag="", slow=False):
        wait = list(wait) + [(sname, self.val(sname)) for sname, _ in inc]
        if slow:
            self.op(eng, lambda e, o=out, i=in_: e.dma_start(
                out=o, in_=i, allow_slow_non_contiguous=True),
                wait=wait, inc=inc, tag=tag)
        else:
            self.op(eng, lambda e, o=out, i=in_: e.dma_start(out=o, in_=i),
                    wait=wait, inc=inc, tag=tag)

    def emit(self, block):
        for eng in self.ENGS:
            def mk(eng):
                def body(e):
                    for fn, wait, inc, tag in self.items[eng]:
                        for s, v in wait:
                            e.wait_ge(self.sems[s], v)
                        ins = fn(e)
                        if inc:
                            assert ins is not None, f"inc on None ({tag})"
                            for s, n in inc:
                                ins = ins.then_inc(self.sems[s], n)
                    if "done" in self.sems:
                        e.wait_ge(self.sems["done"], self.cnt["done"])
                return body
            getattr(block, eng)(mk(eng))


def rope_tables():
    import jax
    import jax.numpy as jnp
    with jax.default_device(jax.devices("cpu")[0]):
        half = DH // 2
        inv_freq = 10000.0 ** (-jnp.arange(half, dtype=jnp.float32) / half)
        pos = jnp.arange(L, dtype=jnp.float32)
        ang = pos[:, None] * inv_freq[None, :]
        cos = np.asarray(jnp.cos(ang))
        sin = np.asarray(jnp.sin(ang))
    COS = np.concatenate([cos.T, cos.T], axis=0)
    SINP = np.concatenate([-sin.T, sin.T], axis=0)
    return np.ascontiguousarray(COS), np.ascontiguousarray(SINP)


def tile_w(wT):
    D, O = wT.shape
    t = wT.reshape(D // 128, 128, O // 128, 128)
    return np.ascontiguousarray(t.transpose(2, 1, 0, 3).reshape(O // 128, 128, D))


def prep_inputs(inp):
    x = np.asarray(inp["L0_attn_x"], dtype=np.float32)
    COS, SINP = rope_tables()
    T = lambda a: np.ascontiguousarray(np.asarray(a, np.float32).T)
    wqt = tile_w(T(inp["L0_attn_wq"]))
    wkt = tile_w(T(inp["L0_attn_wk"]))
    wvt = tile_w(T(inp["L0_attn_wv"]))
    wot = tile_w(T(inp["L0_attn_wo"]))
    w1t = tile_w(T(inp["L0_ffn_w1"]))
    w2t = tile_w(T(inp["L0_ffn_w2"]))
    w3t = tile_w(T(inp["L0_ffn_w3"]))
    rs = lambda k, n: np.ascontiguousarray(
        np.asarray(inp[k], np.float32).reshape(n, 128).T)
    bq, bk, bv = rs("L0_attn_bq", 16), rs("L0_attn_bk", 2), rs("L0_attn_bv", 2)
    g1, g2, g3 = rs("L0_ffn_g1", 16), rs("L0_ffn_g2", 16), rs("L0_ffn_g3", 64)
    iden = np.eye(128, dtype=np.float32)
    cst = np.tile(np.array([[SM_SCALE, EXP_BIAS, -0.5, 0.5]], dtype=np.float32),
                  (128, 1))
    shared = dict(wqt=wqt, wkt=wkt, wvt=wvt, wot=wot, w1t=w1t, w2t=w2t,
                  w3t=w3t, bq=bq, bk=bk, bv=bv, g1=g1, g2=g2, g3=g3,
                  iden=iden, cst=cst, cosk=COS, sink=SINP)
    in_maps = []
    for c in range(8):
        b, par = c >> 1, c & 1
        tiles = Q_TILES[par]
        qcols = np.concatenate([np.arange(g * 512, (g + 1) * 512) for g in tiles])
        xT = np.ascontiguousarray(x[b].T)
        m = dict(shared)
        m.update(xt=xT,
                 xq=np.ascontiguousarray(xT[:, qcols]),
                 cosq=np.ascontiguousarray(COS[:, qcols]),
                 sinq=np.ascontiguousarray(SINP[:, qcols]))
        thr = np.zeros((128, 32), dtype=np.float32)
        for s in range(4):
            q0 = tiles[s] * 512
            for i in range(8):
                j = CAPS[s] - 8 + i
                thr[:, s * 8 + i] = float(q0 - j * 128)
        m["thr"] = thr
        in_maps.append(m)
    return in_maps


def build(phases=99):
    nc = bass.Bass("TRN2", target_bir_lowering=False,
                   detect_race_conditions=False)
    A = lambda name, shape, dt=F32, kind="ExternalInput": \
        nc.dram_tensor(name, shape, dt, kind=kind).ap()

    xt = A("xt", [E, L]); xq = A("xq", [E, NT])
    cosk = A("cosk", [128, L]); sink = A("sink", [128, L])
    cosq = A("cosq", [128, NT]); sinq = A("sinq", [128, NT])
    wqt = A("wqt", [16, 128, 2048]); wkt = A("wkt", [2, 128, 2048])
    wvt = A("wvt", [2, 128, 2048]); wot = A("wot", [16, 128, 2048])
    w1t = A("w1t", [64, 128, 2048]); w2t = A("w2t", [64, 128, 2048])
    w3t = A("w3t", [16, 128, 8192])
    bq = A("bq", [128, 16]); bk = A("bk", [128, 2]); bv = A("bv", [128, 2])
    g1 = A("g1", [128, 16]); g2 = A("g2", [128, 16]); g3 = A("g3", [128, 64])
    iden = A("iden", [128, 128]); cst = A("cst", [128, 4])
    thr = A("thr", [128, 32])
    out = A("out", [E, NT], kind="ExternalOutput")

    x1pre_hbm = A("x1pre_hbm", [E, NT], kind="Internal")
    xhat_hbm = A("xhat_hbm", [E, NT], kind="Internal")
    x2_hbm = A("x2_hbm", [E, NT], BF16, kind="Internal")
    y_hbm = A("y_hbm", [2, E, NT], BF16, kind="Internal")
    psi_hbm = A("psi_hbm", [4, H, 512], BF16, kind="Internal")
    t3_hbm = A("t3_hbm", [16, 128, 8192], BF16, kind="Internal")
    m2x_hbm = A("m2x_hbm", [128, 16], BF16, kind="Internal")

    dbg = {}
    if DEBUG:
        for k, shp in {"kq": [256, L], "vv": [NKC * 128, 256],
                       "qrope": [E, NT], "attno": [E, NT], "x1pre": [E, NT],
                       "xhat": [E, NT],
                       "y1": [E, NT], "y2": [E, NT], "rows": [8, NT],
                       "srow": [4, 4], "x2": [E, NT]}.items():
            dbg[k] = A("d_" + k, shp, kind="ExternalOutput")

    gctx = contextlib.ExitStack()
    SB = lambda c, name, shape, dt=F32: c.enter_context(nc.sbuf_tensor(name, shape, dt))
    PS = lambda c, name, shape, dt=F32: c.enter_context(nc.psum_tensor(name, shape, dt))

    with gctx:
        cstt = SB(gctx, "cstt", [128, 4])
        idb = SB(gctx, "idb", [128, 128], BF16)
        thrt = SB(gctx, "thrt", [128, 32])
        onesb = SB(gctx, "onesb", [128, 1], BF16)
        onesr = SB(gctx, "onesr", [1, 128])
        bqt = SB(gctx, "bqt", [128, 16]); bkt = SB(gctx, "bkt", [128, 2])
        bvt = SB(gctx, "bvt", [128, 2])
        g1t = SB(gctx, "g1t", [128, 16]); g2t = SB(gctx, "g2t", [128, 16])
        g3t = SB(gctx, "g3t", [128, 64])
        g1sq = SB(gctx, "g1sq", [128, 16]); g2sq = SB(gctx, "g2sq", [128, 16])
        g3sq = SB(gctx, "g3sq", [128, 64])
        cells = SB(gctx, "cells", [1, 64])
        csnap = SB(gctx, "csnap", [1, 32])
        C = lambda i: cells[0:1, i:i + 1]
        C_S1, C_S2, C_S3, C_A1, C_A2, C_A3, C_TH1, C_TH2, C_TH3, C_T0, C_T1, \
            C_T2 = range(12)
        thcol = SB(gctx, "thcol", [128, 8])

        nc.const_aps.register(F32, SM_SCALE, cstt[:, 0:1])
        nc.const_aps.register(F32, EXP_BIAS, cstt[:, 1:2])
        nc.const_aps.register(F32, -0.5, cstt[:, 2:3])
        nc.const_aps.register(F32, 0.5, cstt[:, 3:4])

        S = Sched(nc)
        S.sem("done")
        ld = "ld"

        for dst, src in [(cstt[:], cst), (thrt[:], thr),
                         (bqt[:], bq), (bkt[:], bk), (bvt[:], bv),
                         (g1t[:], g1), (g2t[:], g2), (g3t[:], g3)]:
            S.dma("sync", dst, src, inc=[(ld, 16)])
        p0_ld = S.val(ld)
        S.dma("gpsimd", idb[:], iden, inc=[("gid", 16)])
        gid_ld = S.val("gid")
        S.op("vector", lambda e: e.memset(onesb[:], 1.0), inc=[("v0", 1)])
        S.op("vector", lambda e: e.memset(onesr[:], 1.0), inc=[("v0", 1)])
        S.op("vector", lambda e: e.tensor_mul(g1sq[:], g1t[:], g1t[:]),
             wait=[(ld, p0_ld)], inc=[("v0", 1)])
        S.op("vector", lambda e: e.tensor_mul(g2sq[:], g2t[:], g2t[:]),
             inc=[("v0", 1)])
        S.op("vector", lambda e: e.tensor_mul(g3sq[:], g3t[:], g3t[:]),
             inc=[("v0", 1)])
        v0_base = S.val("v0")

        prctx = contextlib.ExitStack()
        prows = SB(prctx, "prows", [1, 4 * NT])
        PR = lambda i: prows[0:1, i * NT:(i + 1) * NT]
        PA, PB, PC, PD = 0, 1, 2, 3

        actx = contextlib.ExitStack()
        arena = SB(actx, "arena", [128, 17 * NT], BF16)
        ASL = lambda i: arena[:, i * NT:(i + 1) * NT]
        D_iota = SB(actx, "D_iota", [128, 512])
        S.op("gpsimd", lambda e: e.iota(
            D_iota[:], pattern=[[1, 512]], base=0, channel_multiplier=-1,
            allow_small_or_imprecise_dtypes=True), inc=[("g0", 1)])

        with contextlib.ExitStack() as p3a:
            xqb = SB(p3a, "xqb", [128, 16 * NT], BF16)
            for dc in range(16):
                S.dma("gpsimd", xqb[:, dc * NT:(dc + 1) * NT],
                      xq[dc * 128:(dc + 1) * 128, :], inc=[("lxq", 16)])
            xq_ld = S.val("lxq")
            ckq = SB(p3a, "ckq", [128, NT]); skq = SB(p3a, "skq", [128, NT])
            S.dma("sync", ckq[:], cosq, inc=[(ld, 16)])
            S.dma("sync", skq[:], sinq, inc=[(ld, 16)])
            ld_q = S.val(ld)
            wqb = SB(p3a, "wqb", [128, 2048], BF16)
            qf = SB(p3a, "qf", [128, NT])
            qsw = SB(p3a, "qsw", [128, NT // 2])
            pq = PS(p3a, "pq", [128, 2 * 512])
            for h in range(16):
                S.dma("gpsimd", wqb[:], wqt[h],
                      wait=[("tq", S.val("tq"))], inc=[("lq", 16)])
                for sl in range(4):
                    def qmm(e, sl=sl):
                        last = None
                        for dc in range(16):
                            last = e.matmul(
                                pq[:, (sl % 2) * 512:(sl % 2 + 1) * 512],
                                wqb[:, dc * 128:(dc + 1) * 128],
                                xqb[:, dc * NT + sl * 512:dc * NT + (sl + 1) * 512],
                                start=(dc == 0), stop=(dc == 15))
                        return last
                    S.op("tensor", qmm,
                         wait=[("lq", S.val("lq")), ("lxq", xq_ld),
                               ("sq", max(0, S.val("sq") - 1))] +
                              ([("vq2", 6 * h)] if sl == 0 else []),
                         inc=[("tq", 1)], tag=f"qmm{h}_{sl}")
                    S.op("scalar", lambda e, sl=sl, h=h: e.activation(
                        qf[:, sl * 512:(sl + 1) * 512],
                        pq[:, (sl % 2) * 512:(sl % 2 + 1) * 512], AF.Identity,
                        bias=bqt[:, h:h + 1]),
                        wait=[("tq", S.val("tq")), (ld, p0_ld)], inc=[("sq", 1)])
                sq_h = S.val("sq")
                for qh in range(2):
                    cs = slice(qh * (NT // 2), (qh + 1) * (NT // 2))
                    S.dma("gpsimd", qsw[0:64, :], qf[64:128, cs],
                          wait=[("sq", sq_h), ("vq2", S.val("vq2"))],
                          inc=[("gq", 16)])
                    S.dma("gpsimd", qsw[64:128, :], qf[0:64, cs],
                          inc=[("gq", 16)])
                    S.op("vector", lambda e, cs=cs: e.tensor_mul(
                        qf[:, cs], qf[:, cs], ckq[:, cs]),
                        wait=[("sq", sq_h), (ld, ld_q), ("gq", S.val("gq"))],
                        inc=[("vq2", 1)])
                    S.op("vector", lambda e, cs=cs: e.tensor_mul(
                        qsw[:], qsw[:], skq[:, cs]), inc=[("vq2", 1)])
                    S.op("vector", lambda e, h=h, cs=cs: e.tensor_add(
                        ASL(h)[:, cs], qf[:, cs], qsw[:]),
                        inc=[("vq2", 1)], tag=f"qrope{h}_{qh}")
        q_done = S.val("vq2")
        S.barrier()

        if DEBUG and phases == 2:
            with contextlib.ExitStack() as pd:
                qdf = SB(pd, "qdf", [128, NT])
                for h in range(16):
                    S.op("vector", lambda e, h=h: e.tensor_copy(qdf[:], ASL(h)),
                         wait=[("vq2", q_done), ("od", S.val("od"))],
                         inc=[("vd", 1)])
                    S.dma("sync", dbg["qrope"][h * 128:(h + 1) * 128, :], qdf[:],
                          wait=[("vd", S.val("vd"))], inc=[("od", 16)])
                S.dma("sync", out[0:128, :], xq[0:128, :], inc=[("od", 16)])
                S.op("sync", lambda e: e.nop(), wait=[("od", S.val("od"))],
                     inc=[("done", 1)])
            actx.close(); prctx.close()
            with nc.Block() as block:
                S.emit(block)
            return nc

        kvctx = contextlib.ExitStack()
        kb = SB(kvctx, "kb", [128, 2 * L], BF16)
        vtok = SB(kvctx, "vtok", [128, NKC * 256], BF16)

        with contextlib.ExitStack() as p1:
            wkb = SB(p1, "wkb", [128, 16 * 256], BF16)
            wvb = SB(p1, "wvb", [128, 16 * 256], BF16)
            xkb = SB(p1, "xkb", [128, 16 * 512], BF16)
            kfh = SB(p1, "kfh", [128, NT])
            vbfh = SB(p1, "vbfh", [128, NT], BF16)
            ckh = SB(p1, "ckh", [128, NT]); skh = SB(p1, "skh", [128, NT])
            ksw = SB(p1, "ksw", [128, NT])
            pkv = PS(p1, "pkv", [128, 2 * 512])
            ptr = PS(p1, "ptr", [128, 128], BF16)
            for wsrc, wdst in [(wkt, wkb), (wvt, wvb)]:
                for oc in range(2):
                    def _wdst_ap(wdst=wdst, oc=oc):
                        return wdst[:].rearrange("p (a b) -> p a b", b=128)[:, :, :]
                    for dc in range(16):
                        S.dma("gpsimd",
                              wdst[:, dc * 256 + oc * 128:dc * 256 + oc * 128 + 128],
                              wsrc[oc, :, dc * 128:(dc + 1) * 128],
                              inc=[("lw", 16)])
            v0_kv = S.val("vw")
            for kvh in range(2):
                for half in range(2):
                    for sl4 in range(4):
                        sl = half * 4 + sl4
                        for dc in range(16):
                            S.dma("gpsimd", xkb[:, dc * 512:(dc + 1) * 512],
                                  xt[dc * 128:(dc + 1) * 128, sl * 512:(sl + 1) * 512],
                                  wait=([("tkv", S.val("tkv"))] if dc == 0 else []),
                                  inc=[("lx", 16)])
                        def mm_slab(e, kvh=kvh):
                            last = None
                            for w2 in range(2):
                                wsel = wkb if w2 == 0 else wvb
                                for dc in range(16):
                                    lt = wsel[:, dc * 256 + kvh * 128:dc * 256 + kvh * 128 + 128]
                                    last = e.matmul(
                                        pkv[:, w2 * 512:(w2 + 1) * 512], lt,
                                        xkb[:, dc * 512:(dc + 1) * 512],
                                        start=(dc == 0), stop=(dc == 15))
                            return last
                        S.op("tensor", mm_slab,
                             wait=[("lx", S.val("lx")), ("lw", S.val("lw")),
                                   ("skv", S.val("skv"))],
                             inc=[("tkv", 1)], tag=f"mmkv{kvh}_{sl}")
                        def evac(e, kvh=kvh, sl4=sl4):
                            e.activation(kfh[:, sl4 * 512:(sl4 + 1) * 512],
                                         pkv[:, 0:512], AF.Identity,
                                         bias=bkt[:, kvh:kvh + 1])
                            return e.activation(vbfh[:, sl4 * 512:(sl4 + 1) * 512],
                                                pkv[:, 512:1024], AF.Identity,
                                                bias=bvt[:, kvh:kvh + 1])
                        S.op("scalar", evac,
                             wait=[("tkv", S.val("tkv")), (ld, p0_ld),
                                   ("vkr", S.val("vkr")),
                                   ("vtr", S.val("vtr"))],
                             inc=[("skv", 1)], tag=f"evkv{kvh}_{sl}")
                    skv_h = S.val("skv")
                    S.dma("gpsimd", ksw[0:64, :], kfh[64:128, :],
                          wait=[("skv", skv_h), ("vkr", S.val("vkr"))],
                          inc=[("gkd", 16)])
                    S.dma("gpsimd", ksw[64:128, :], kfh[0:64, :],
                          inc=[("gkd", 16)])
                    cs = slice(half * NT, (half + 1) * NT)
                    S.dma("sync", ckh[:], cosk[:, cs],
                          wait=[("vkr", S.val("vkr"))], inc=[(ld, 16)])
                    S.dma("sync", skh[:], sink[:, cs], inc=[(ld, 16)])
                    ld_h = S.val(ld)
                    S.op("vector", lambda e: e.tensor_mul(kfh[:], kfh[:], ckh[:]),
                         wait=[("skv", skv_h), (ld, ld_h),
                               ("gkd", S.val("gkd"))], inc=[("vkr", 1)])
                    S.op("vector", lambda e: e.tensor_mul(ksw[:], ksw[:], skh[:]),
                         inc=[("vkr", 1)])
                    S.op("vector", lambda e, kvh=kvh, cs=cs: e.tensor_add(
                        kb[:, kvh * L:(kvh + 1) * L][:, cs], kfh[:], ksw[:]),
                        inc=[("vkr", 1)], tag=f"ropek{kvh}_{half}")
                    for kc4 in range(16):
                        kc = half * 16 + kc4
                        S.op("tensor", lambda e, kc4=kc4: e.transpose(
                            ptr[:], vbfh[:, kc4 * 128:(kc4 + 1) * 128], idb[:]),
                            wait=[("skv", skv_h), ("gid", gid_ld),
                                  ("vtr", S.val("vtr"))], inc=[("ttr", 1)])
                        S.op("vector", lambda e, kc=kc, kvh=kvh: e.tensor_copy(
                            vtok[:, kc * 256 + kvh * 128:kc * 256 + (kvh + 1) * 128],
                            ptr[:]), wait=[("ttr", S.val("ttr"))],
                            inc=[("vtr", 1)])
        kv_done = [("vkr", S.val("vkr")), ("vtr", S.val("vtr"))]
        S.barrier()

        if DEBUG and phases == 1:
            with contextlib.ExitStack() as pd:
                kqf = SB(pd, "kqf", [128, L])
                for kvh in range(2):
                    S.op("vector", lambda e, kvh=kvh: e.tensor_copy(
                        kqf[:], kb[:, kvh * L:(kvh + 1) * L]),
                        wait=kv_done + [("od", S.val("od"))], inc=[("vd", 1)])
                    S.dma("sync", dbg["kq"][kvh * 128:(kvh + 1) * 128, :], kqf[:],
                          wait=[("vd", S.val("vd"))], inc=[("od", 16)])
                vvf = SB(pd, "vvf", [128, 256])
                for kc in range(NKC):
                    S.op("vector", lambda e, kc=kc: e.tensor_copy(
                        vvf[:], vtok[:, kc * 256:(kc + 1) * 256]),
                        wait=kv_done + [("od", S.val("od"))], inc=[("vd", 1)])
                    S.dma("sync", dbg["vv"][kc * 128:(kc + 1) * 128, :], vvf[:],
                          wait=[("vd", S.val("vd"))], inc=[("od", 16)])
                S.dma("sync", out[0:128, :], xq[0:128, :], inc=[("od", 16)])
                S.op("sync", lambda e: e.nop(), wait=[("od", S.val("od"))],
                     inc=[("done", 1)])
            kvctx.close(); actx.close(); prctx.close()
            with nc.Block() as block:
                S.emit(block)
            return nc

        with contextlib.ExitStack() as p2:
            wsl = [SB(p2, f"wsl{i}", [128, 2048]) for i in range(2)]
            wab = [SB(p2, f"wab{i}", [128, 2048], BF16) for i in range(2)]
            pab = PS(p2, "pab", [128, 512])
            abrow = SB(p2, "abrow", [1, 3 * 512])
            S.op("vector", lambda e: e.memset(abrow[:], 0.0), inc=[("vab", 1)])
            slabs = []
            for wi, wt_ap in enumerate([w1t, w2t]):
                for hc in range(64):
                    slabs.append((wi, wt_ap[hc]))
            for oc in range(16):
                for q in range(4):
                    slabs.append((2, w3t[oc, :, q * 2048:(q + 1) * 2048]))
            for si, (wi, src) in enumerate(slabs):
                bi = si % 2
                S.dma("sync", wsl[bi][:], src,
                      wait=[("vab", max(0, S.val("vab") - 2))], inc=[("lab", 16)])
                S.op("scalar", lambda e, bi=bi: e.activation(
                    wab[bi][:], wsl[bi][:], AF.Abs),
                    wait=[("lab", S.val("lab")),
                          ("tab", max(0, S.val("tab") - 1))],
                    inc=[("vab", 1)], tag=f"abs{si}")
                def absmm(e, bi=bi):
                    last = None
                    for s4 in range(4):
                        last = e.matmul(pab[0:1, 0:512], onesb[:],
                                        wab[bi][:, s4 * 512:(s4 + 1) * 512],
                                        start=(s4 == 0), stop=(s4 == 3))
                    return last
                S.op("tensor", absmm,
                     wait=[("vab", S.val("vab")), ("v0", v0_base),
                           ("vab2", S.val("vab2"))], inc=[("tab", 1)])
                S.op("vector", lambda e, wi=wi: e.scalar_tensor_tensor(
                    abrow[0:1, wi * 512:(wi + 1) * 512], pab[0:1, 0:512], 1.0,
                    abrow[0:1, wi * 512:(wi + 1) * 512], ALU.mult, ALU.add),
                    wait=[("tab", S.val("tab"))], inc=[("vab2", 1)],
                    tag=f"abr{si}")
            def fin_rows(e):
                for wi in range(3):
                    e.reduce_sum(C(C_T0 + wi), abrow[0:1, wi * 512:(wi + 1) * 512],
                                 axis=AX.X)
                e.tensor_scalar_mul(C(C_S1), C(C_T0), 1.0 / (H * E))
                e.tensor_scalar_mul(C(C_S2), C(C_T1), 1.0 / (H * E))
                return e.tensor_scalar_mul(C(C_S3), C(C_T2), 1.0 / (H * E))
            S.op("vector", fin_rows, wait=[("vab2", S.val("vab2"))],
                 inc=[("vab3", 1)])
            def fin2a(e):
                last = None
                for i, (cs, cth) in enumerate([(C_S1, C_TH1), (C_S2, C_TH2),
                                               (C_S3, C_TH3)]):
                    e.tensor_scalar(C(cth), C(cs), 1e-8, ATANH_HALF, ALU.add,
                                    ALU.mult)
                    e.tensor_scalar(C(C_T0 + 3 + i), C(cs), 1.0, None, ALU.add)
                    last = e.tensor_scalar(C(C_T0 + 6 + i), C(cs), -1.0, 1.0,
                                           ALU.mult, ALU.add)
                return last
            S.op("vector", fin2a, inc=[("vab3", 1)])
            def fin2b(e):
                last = None
                for i in range(3):
                    last = e.reciprocal(C(C_T0 + 9 + i), C(C_T0 + 6 + i))
                return last
            S.op("vector", fin2b, inc=[("vab3", 1)])
            def fin2c(e):
                last = None
                for i, ca in enumerate((C_A1, C_A2, C_A3)):
                    last = e.tensor_mul(C(ca), C(C_T0 + 3 + i), C(C_T0 + 9 + i))
                return last
            S.op("vector", fin2c, wait=[("vab3", S.val("vab3"))],
                 inc=[("vab3", 1)])
            S.op("vector", lambda e: e.tensor_copy(csnap[0:1, 0:16],
                                                    cells[0:1, 0:16]),
                 wait=[("vab3", S.val("vab3"))], inc=[("vab3", 1)])
            def fin3(e):
                last = None
                for ca in (C_A1, C_A2, C_A3):
                    last = e.activation(C(ca), C(ca), AF.Ln)
                return last
            S.op("scalar", fin3, wait=[("vab3", S.val("vab3"))], inc=[("sab", 1)])
            S.op("vector", lambda e: e.tensor_copy(csnap[0:1, 16:32],
                                                   cells[0:1, 0:16]),
                 wait=[("sab", S.val("sab"))], inc=[("vab3", 1)])
            def fin4(e):
                last = None
                for ca in (C_A1, C_A2, C_A3):
                    last = e.tensor_scalar_mul(C(ca), C(ca), 0.5)
                return last
            S.op("vector", fin4, wait=[("sab", S.val("sab"))], inc=[("vab3", 1)])
            pbc1 = PS(p2, "pbc1", [128, 8])
            def thbc_mm(e):
                last = None
                for wi, cth in enumerate((C_TH1, C_TH2, C_TH3)):
                    last = e.matmul(pbc1[:, 2 * wi:2 * wi + 1], onesr[:], C(cth),
                                    start=True, stop=True)
                return last
            S.op("tensor", thbc_mm, wait=[("vab3", S.val("vab3")),
                                          ("v0", v0_base)], inc=[("tab", 1)])
            def thbc_fin(e):
                e.tensor_copy(thcol[:, 0:1], pbc1[:, 0:1])
                e.tensor_copy(thcol[:, 2:3], pbc1[:, 2:3])
                e.tensor_copy(thcol[:, 4:5], pbc1[:, 4:5])
                e.tensor_scalar_mul(thcol[:, 1:2], pbc1[:, 0:1], -1.0)
                e.tensor_scalar_mul(thcol[:, 3:4], pbc1[:, 2:3], -1.0)
                return e.tensor_scalar_mul(thcol[:, 5:6], pbc1[:, 4:5], -1.0)
            S.op("vector", thbc_fin, wait=[("tab", S.val("tab"))],
                 inc=[("vab3", 1)])
        p2_done = [("vab3", S.val("vab3"))]
        S.barrier()

        with contextlib.ExitStack() as p3b:
            masks = SB(p3b, "masks", [128, 32 * 512], BF16)
            def genmasks(e):
                last = None
                for u in range(32):
                    last = e.tensor_scalar(
                        masks[:, u * 512:(u + 1) * 512], D_iota[:],
                        thrt[:, u:u + 1], 0.0, ALU.add, ALU.is_ge)
                return last
            S.op("vector", genmasks, wait=[("g0", 1), (ld, p0_ld)],
                 inc=[("vmk", 1)])
            w3sl = SB(p3b, "w3sl", [128, 2048])
            w3tp = SB(p3b, "w3tp", [128, 2048], BF16)
            t3b = SB(p3b, "t3b", [128, 2048], BF16)

            def p2b_block(qi):
                oc, qq = qi // 4, qi % 4
                cs = slice(qq * 2048, (qq + 1) * 2048)
                S.dma("sync", w3sl[:], w3t[oc, :, cs],
                      wait=[("vt3", S.val("vt3"))], inc=[("lt3", 16)])
                S.op("vector", lambda e: e.tensor_scalar(
                    w3tp[:], w3sl[:], thcol[:, 4:5], None, ALU.is_ge),
                    wait=[("lt3", S.val("lt3")), ("gt3", S.val("gt3"))] +
                         (p2_done if qi == 0 else []),
                    inc=[("vt3", 1)])
                S.op("vector", lambda e, oc=oc: e.scalar_tensor_tensor(
                    t3b[:], w3sl[:], thcol[:, 5:6], w3tp[:], ALU.is_lt,
                    ALU.subtract), inc=[("vt3", 1)], tag=f"t3_{qi}")
                S.dma("gpsimd", t3_hbm[oc, :, cs], t3b[:],
                      wait=[("vt3", S.val("vt3"))], inc=[("gt3", 16)])

            ebuf = [SB(p3b, f"ebuf{i}", [128, 2 * 512], BF16) for i in range(2)]
            esum = SB(p3b, "esum", [128, 512], BF16)
            rrow = SB(p3b, "rrow", [1, 512])
            rbc = SB(p3b, "rbc", [128, 512])
            psc = [PS(p3b, f"psc{i}", [128, 2 * 512]) for i in range(2)]
            pot = [PS(p3b, f"pot{i}", [128, 512]) for i in range(2)]
            pdn = PS(p3b, "pdn", [1, 512])
            prb = PS(p3b, "prb", [128, 512])
            vmk = S.val("vmk")
            vnr0 = S.val("vnr")
            unit = 0
            for h in range(16):
                kvh = h // 8
                qsl = ASL(h)
                outsl = ASL((h + 16) % 17)
                for s in range(4):
                    cap = CAPS[s]
                    ngr = cap // 2
                    q0, q1 = s * 512, (s + 1) * 512
                    upp = unit % 2
                    tsc_after = {}

                    def emit_smm(g):
                        pbi = g % 2
                        def smm(e, g=g, pbi=pbi, kvh=kvh, qsl=qsl, q0=q0, q1=q1):
                            last = None
                            for j in range(2):
                                kc = g * 2 + j
                                last = e.matmul(
                                    psc[pbi][:, j * 512:(j + 1) * 512],
                                    kb[:, kvh * L + kc * 128:kvh * L + (kc + 1) * 128],
                                    qsl[:, q0:q1], start=True, stop=True)
                            return last
                        S.op("tensor", smm,
                             wait=[("se", S.val("se")), ("vq2", q_done)] +
                                  (kv_done if h == 0 and s == 0 and g == 0 else []),
                             inc=[("tsc", 1)], tag=f"smm{h}_{s}_{g}")
                        tsc_after[g] = S.val("tsc")

                    emit_smm(0)
                    for g in range(ngr):
                        pbi = g % 2
                        if g + 1 < ngr:
                            emit_smm(g + 1)
                        S.op("scalar", lambda e, pbi=pbi: e.activation(
                            ebuf[pbi][:], psc[pbi][:], AF.Exp, bias=EXP_BIAS,
                            scale=SM_SCALE),
                            wait=[("tsc", tsc_after[g]),
                                  ("tav", max(0, S.val("tav") - 1))],
                            inc=[("se", 1)], tag=f"exp{h}_{s}_{g}")
                        def vstep(e, g=g, pbi=pbi, s=s, cap=cap):
                            for j in range(2):
                                kc = g * 2 + j
                                if kc >= cap - 8:
                                    u = s * 8 + (kc - (cap - 8))
                                    e.tensor_mul(
                                        ebuf[pbi][:, j * 512:(j + 1) * 512],
                                        ebuf[pbi][:, j * 512:(j + 1) * 512],
                                        masks[:, u * 512:(u + 1) * 512])
                            return e.tensor_add(esum[:], ebuf[pbi][:, 0:512],
                                                ebuf[pbi][:, 512:1024])
                        S.op("vector", vstep,
                             wait=[("se", S.val("se")), ("vmk", vmk),
                                   ("tav", S.val("tav"))],
                             inc=[("ve", 1)], tag=f"vm{h}_{s}_{g}")
                        def avmm(e, g=g, pbi=pbi, kvh=kvh, ngr=ngr, upp=upp):
                            e.matmul(pdn[0:1, :], onesb[:], esum[:],
                                     start=(g == 0), stop=(g == ngr - 1))
                            last = None
                            for j in range(2):
                                kc = g * 2 + j
                                last = e.matmul(
                                    pot[upp][:], vtok[:, kc * 256 + kvh * 128:
                                                      kc * 256 + (kvh + 1) * 128],
                                    ebuf[pbi][:, j * 512:(j + 1) * 512],
                                    start=(g == 0 and j == 0),
                                    stop=(g == ngr - 1 and j == 1))
                            return last
                        av_wait = [("ve", S.val("ve"))]
                        if g == 0 and unit >= 2:
                            av_wait.append(("vnr", vnr0 + 3 * (unit - 1)))
                        S.op("tensor", avmm, wait=av_wait,
                             inc=[("tav", 1)], tag=f"av{h}_{s}_{g}")
                    S.op("vector", lambda e: e.reciprocal(rrow[:], pdn[0:1, :]),
                         wait=[("tav", S.val("tav"))], inc=[("vnr", 1)])
                    S.op("tensor", lambda e: e.matmul(
                        prb[:], onesr[:], rrow[:], start=True, stop=True),
                        wait=[("vnr", S.val("vnr"))], inc=[("trb", 1)])
                    S.op("vector", lambda e: e.tensor_copy(rbc[:], prb[:]),
                         wait=[("trb", S.val("trb"))], inc=[("vnr", 1)])
                    S.op("vector", lambda e, outsl=outsl, q0=q0, q1=q1, upp=upp:
                         e.tensor_mul(outsl[:, q0:q1], pot[upp][:], rbc[:]),
                         inc=[("vnr", 1)], tag=f"on{h}_{s}")
                    p2b_block(unit)
                    unit += 1
        att_done = S.val("vnr")
        S.barrier()
        kvctx.close()

        if DEBUG and phases == 4:
            with contextlib.ExitStack() as pd:
                adf = SB(pd, "adf", [128, NT])
                for h in range(16):
                    S.op("vector", lambda e, h=h: e.tensor_copy(
                        adf[:], ASL((h + 16) % 17)),
                        wait=[("vnr", att_done), ("od", S.val("od"))],
                        inc=[("vd", 1)])
                    S.dma("sync", dbg["attno"][h * 128:(h + 1) * 128, :], adf[:],
                          wait=[("vd", S.val("vd"))], inc=[("od", 16)])
                S.dma("sync", out[0:128, :], xq[0:128, :], inc=[("od", 16)])
                S.op("sync", lambda e: e.nop(), wait=[("od", S.val("od"))],
                     inc=[("done", 1)])
            actx.close(); prctx.close()
            with nc.Block() as block:
                S.emit(block)
            return nc

        with contextlib.ExitStack() as p4:
            wob = SB(p4, "wob", [128, 2048], BF16)
            xstg3 = [SB(p4, f"xstg3{i}", [128, NT]) for i in range(2)]
            x1c = SB(p4, "x1c", [128, NT])
            x1bf = SB(p4, "x1bf", [128, NT], BF16)
            sqbf = SB(p4, "sqbf", [128, NT], BF16)
            pwo = [PS(p4, f"pwo{i}", [128, 512]) for i in range(2)]
            pst = PS(p4, "pst", [1, 512])
            S.op("vector", lambda e: e.memset(PR(PA), 0.0), inc=[("vws", 1)])
            S.op("vector", lambda e: e.memset(PR(PB), 0.0), inc=[("vws", 1)])
            for oc in range(16):
                bi = oc % 2
                S.dma("gpsimd", wob[:], wot[oc],
                      wait=[("two", S.val("two"))], inc=[("lwo", 16)])
                S.dma("sync", xstg3[bi][:], xq[oc * 128:(oc + 1) * 128, :],
                      wait=[("vx1", max(0, S.val("vx1") - 4))], inc=[("lx3", 16)])
                for sl in range(4):
                    def womm(e, sl=sl):
                        last = None
                        for hc in range(16):
                            src = ASL((hc + 16) % 17)
                            last = e.matmul(
                                pwo[sl % 2][:], wob[:, hc * 128:(hc + 1) * 128],
                                src[:, sl * 512:(sl + 1) * 512],
                                start=(hc == 0), stop=(hc == 15))
                        return last
                    S.op("tensor", womm,
                         wait=[("lwo", S.val("lwo")), ("vnr", att_done),
                               ("vx1", max(0, S.val("vx1") - 1))],
                         inc=[("two", 1)], tag=f"womm{oc}_{sl}")
                    S.op("vector", lambda e, sl=sl, bi=bi: e.tensor_add(
                        x1c[:, sl * 512:(sl + 1) * 512], pwo[sl % 2][:],
                        xstg3[bi][:, sl * 512:(sl + 1) * 512]),
                        wait=[("two", S.val("two")), ("lx3", S.val("lx3")),
                              ("gx1", S.val("gx1"))],
                        inc=[("vx1", 1)], tag=f"x1{oc}_{sl}")
                vx1_oc = S.val("vx1")
                S.dma("gpsimd", x1pre_hbm[oc * 128:(oc + 1) * 128, :], x1c[:],
                      wait=[("vx1", vx1_oc)], inc=[("gx1", 16)])
                S.op("vector", lambda e: e.tensor_copy(x1bf[:], x1c[:]),
                     wait=[("vx1", vx1_oc)], inc=[("vx1", 1)])
                S.op("scalar", lambda e: e.activation(sqbf[:], x1bf[:], AF.Square),
                     wait=[("vx1", S.val("vx1"))], inc=[("sx1", 1)])
                for which, (src, ridx) in enumerate([(x1bf, PA), (sqbf, PB)]):
                    for s4 in range(4):
                        S.op("tensor", lambda e, src=src, s4=s4: e.matmul(
                            pst[0:1, :], onesb[:],
                            src[:, s4 * 512:(s4 + 1) * 512], start=True,
                            stop=True),
                            wait=[("sx1", S.val("sx1")), ("vws", S.val("vws"))],
                            inc=[("tst", 1)])
                        S.op("vector", lambda e, ridx=ridx, s4=s4:
                             e.scalar_tensor_tensor(
                                 prows[0:1, ridx * NT + s4 * 512:
                                       ridx * NT + (s4 + 1) * 512],
                                 pst[0:1, :], 1.0,
                                 prows[0:1, ridx * NT + s4 * 512:
                                       ridx * NT + (s4 + 1) * 512],
                                 ALU.mult, ALU.add),
                             wait=[("tst", S.val("tst"))], inc=[("vws", 1)],
                             tag=f"st{oc}_{which}_{s4}")
        wo_done = [("gx1", S.val("gx1")), ("vws", S.val("vws"))]
        S.barrier()
        actx.close()

        if DEBUG and phases == 5:
            with contextlib.ExitStack() as pd:
                xdf = SB(pd, "xdf", [128, NT])
                for dc in range(16):
                    S.dma("sync", xdf[:], x1pre_hbm[dc * 128:(dc + 1) * 128, :],
                          wait=wo_done + [("od", S.val("od"))], inc=[("od", 16)])
                    S.dma("sync", dbg["x1pre"][dc * 128:(dc + 1) * 128, :],
                          xdf[:], wait=[("od", S.val("od"))], inc=[("od", 16)])
                S.dma("sync", dbg["rows"][0:1, :], PR(PA), wait=wo_done,
                      inc=[("od", 16)])
                S.dma("sync", dbg["rows"][1:2, :], PR(PB), inc=[("od", 16)])
                S.dma("sync", out[0:128, :], xq[0:128, :], inc=[("od", 16)])
                S.op("sync", lambda e: e.nop(), wait=[("od", S.val("od"))],
                     inc=[("done", 1)])
            prctx.close()
            with nc.Block() as block:
                S.emit(block)
            return nc

        fctx = contextlib.ExitStack()
        b1bc = SB(fctx, "b1bc", [128, NT], BF16)
        m2r3 = SB(fctx, "m2r3", [128, NT], BF16)

        with contextlib.ExitStack() as p5:
            lrows = SB(p5, "lrows", [1, 6 * NT])
            LR = lambda i: lrows[0:1, i * NT:(i + 1) * NT]
            L_RMS, L_M21, L_M22, L_TA, L_TB, L_K = range(6)
            def ln1rows(e):
                e.tensor_scalar_mul(PR(PA), PR(PA), 1.0 / E)
                e.tensor_scalar_mul(PR(PB), PR(PB), 1.0 / E)
                e.tensor_mul(LR(L_TA), PR(PA), PR(PA))
                e.tensor_sub(PR(PB), PR(PB), LR(L_TA))
                return e.tensor_scalar_add(PR(PB), PR(PB), 1e-5)
            S.op("vector", ln1rows, wait=wo_done, inc=[("vr1", 1)])
            S.op("scalar", lambda e: e.activation(PR(PB), PR(PB), AF.Ln),
                 wait=[("vr1", S.val("vr1"))], inc=[("sr1", 1)])
            S.op("scalar", lambda e: e.activation(PR(PB), PR(PB), AF.Exp,
                                                  scale=-0.5), inc=[("sr1", 1)])
            pbc = PS(p5, "pbc", [128, 512])
            k1bc = SB(p5, "k1bc", [128, NT])
            k2bc = SB(p5, "k2bc", [128, NT])
            with contextlib.ExitStack() as p5a:
                mbc = SB(p5a, "mbc", [128, NT])
                rbc1 = SB(p5a, "rbc1", [128, NT])
                for ridx, dst in [(PA, mbc), (PB, rbc1)]:
                    for s4 in range(4):
                        S.op("tensor", lambda e, ridx=ridx, s4=s4: e.matmul(
                            pbc[:], onesr[:],
                            prows[0:1, ridx * NT + s4 * 512:
                                  ridx * NT + (s4 + 1) * 512],
                            start=True, stop=True),
                            wait=[("sr1", S.val("sr1")), ("vr2", S.val("vr2"))],
                            inc=[("tr1", 1)])
                        S.op("vector", lambda e, dst=dst, s4=s4: e.tensor_copy(
                            dst[:, s4 * 512:(s4 + 1) * 512], pbc[:]),
                            wait=[("tr1", S.val("tr1"))], inc=[("vr2", 1)])
                xstg4 = [SB(p5a, f"xstg4{i}", [128, NT]) for i in range(2)]
                sqb2 = SB(p5a, "sqb2", [128, NT], BF16)
                m2c = SB(p5a, "m2c", [128, NT], BF16)
                m2r1 = SB(p5a, "m2r1", [128, NT], BF16)
                m2r2 = SB(p5a, "m2r2", [128, NT], BF16)
                pst2 = PS(p5a, "pst2", [1, 512])
                S.op("vector", lambda e: e.memset(LR(L_RMS), 0.0), inc=[("vr2", 1)])
                S.op("vector", lambda e: e.memset(m2r1[:], 0.0), inc=[("vr2", 1)])
                S.op("vector", lambda e: e.memset(m2r2[:], 0.0), inc=[("vr2", 1)])
                bc_done = S.val("vr2")
                for dc in range(16):
                    bi = dc % 2
                    S.dma("sync", xstg4[bi][:],
                          x1pre_hbm[dc * 128:(dc + 1) * 128, :],
                          wait=[("vh", max(0, S.val("vh") - 2)),
                                ("sh", max(0, S.val("sh") - 1)),
                                (f"gxh{bi}", S.val(f"gxh{bi}")),
                                ("gx1", S.val("gx1"))], inc=[("lx4", 16)])
                    S.op("vector", lambda e, bi=bi: e.tensor_sub(
                        xstg4[bi][:], xstg4[bi][:], mbc[:]),
                        wait=[("lx4", S.val("lx4")), ("vr2", bc_done),
                              (f"gxh{bi}", S.val(f"gxh{bi}"))],
                        inc=[("vh", 1)])
                    S.op("vector", lambda e, bi=bi: e.tensor_mul(
                        xstg4[bi][:], xstg4[bi][:], rbc1[:]), inc=[("vh", 1)],
                        tag=f"xh{dc}")
                    vh_dc = S.val("vh")
                    S.dma("gpsimd", xhat_hbm[dc * 128:(dc + 1) * 128, :],
                          xstg4[bi][:], wait=[("vh", vh_dc)],
                          inc=[(f"gxh{bi}", 16)])
                    S.op("scalar", lambda e, bi=bi: e.activation(
                        sqb2[:], xstg4[bi][:], AF.Square),
                        wait=[("vh", vh_dc)], inc=[("sh", 1)])
                    for s4 in range(4):
                        S.op("tensor", lambda e, s4=s4: e.matmul(
                            pst2[0:1, :], onesb[:],
                            sqb2[:, s4 * 512:(s4 + 1) * 512], start=True,
                            stop=True),
                            wait=[("sh", S.val("sh")), ("vrm", S.val("vrm"))],
                            inc=[("tst2", 1)])
                        S.op("vector", lambda e, s4=s4: e.scalar_tensor_tensor(
                            lrows[0:1, L_RMS * NT + s4 * 512:
                                  L_RMS * NT + (s4 + 1) * 512],
                            pst2[0:1, :], 1.0,
                            lrows[0:1, L_RMS * NT + s4 * 512:
                                  L_RMS * NT + (s4 + 1) * 512],
                            ALU.mult, ALU.add),
                            wait=[("tst2", S.val("tst2"))], inc=[("vrm", 1)])
                    S.op("vector", lambda e, dc=dc: e.tensor_scalar_mul(
                        m2c[:], sqb2[:], g1sq[:, dc:dc + 1]),
                        wait=[("sh", S.val("sh"))], inc=[("vm2", 1)])
                    S.op("vector", lambda e: e.tensor_max(m2r1[:], m2r1[:],
                                                          m2c[:]),
                         inc=[("vm2", 1)])
                    S.op("vector", lambda e, dc=dc: e.tensor_scalar_mul(
                        m2c[:], sqb2[:], g2sq[:, dc:dc + 1]), inc=[("vm2", 1)])
                    S.op("vector", lambda e: e.tensor_max(m2r2[:], m2r2[:],
                                                          m2c[:]),
                         inc=[("vm2", 1)], tag=f"m2{dc}")
                ptm = PS(p5a, "ptm", [128, 128], BF16)
                mx_tok = SB(p5a, "mx_tok", [128, 16], BF16)
                for which, (m2r, rm2) in enumerate([(m2r1, L_M21),
                                                    (m2r2, L_M22)]):
                    for tcn in range(16):
                        S.op("tensor", lambda e, m2r=m2r, tcn=tcn: e.transpose(
                            ptm[:], m2r[:, tcn * 128:(tcn + 1) * 128], idb[:]),
                            wait=[("vm2", S.val("vm2")), ("vmx", S.val("vmx")),
                                  ("gmx", S.val("gmx"))], inc=[("tmx", 1)])
                        S.op("vector", lambda e, tcn=tcn: e.reduce_max(
                            mx_tok[:, tcn:tcn + 1], ptm[:], axis=AX.X),
                            wait=[("tmx", S.val("tmx"))], inc=[("vmx", 1)])
                    S.dma("gpsimd", m2x_hbm[:], mx_tok[:],
                          wait=[("vmx", S.val("vmx"))], inc=[("gmx", 16)])
                    S.op("gpsimd", lambda e, rm2=rm2: e.dma_start(
                        out=lrows[0:1, rm2 * NT:(rm2 + 1) * NT],
                        in_=m2x_hbm.rearrange("p c -> c p"),
                        allow_slow_non_contiguous=True),
                        wait=[("gmx", S.val("gmx"))], inc=[("gmx", 16)],
                        tag=f"m2row{which}")
            S.op("vector", lambda e: e.tensor_scalar(
                LR(L_RMS), LR(L_RMS), 1.0 / E, 1e-8, ALU.mult, ALU.add),
                wait=[("vrm", S.val("vrm")), ("gmx", S.val("gmx"))],
                inc=[("vr3", 1)])
            S.op("scalar", lambda e: e.activation(LR(L_RMS), LR(L_RMS), AF.Ln),
                 wait=[("vr3", S.val("vr3"))], inc=[("sr2", 1)])
            S.op("scalar", lambda e: e.activation(LR(L_RMS), LR(L_RMS), AF.Exp,
                                                  scale=-0.5), inc=[("sr2", 1)])
            for rm2, ca, kdst, rbeta in [(L_M21, C_A1, k1bc, PA),
                                         (L_M22, C_A2, k2bc, PC)]:
                S.op("scalar", lambda e, rm2=rm2: e.activation(
                    LR(L_TA), LR(rm2), AF.Ln),
                    wait=[("sr2", S.val("sr2")), ("vr4", S.val("vr4"))],
                    inc=[("sr2", 1)])
                S.op("scalar", lambda e: e.activation(
                    LR(L_TA), LR(L_TA), AF.Exp, scale=0.5), inc=[("sr2", 1)])
                def krow_a(e):
                    e.tensor_mul(LR(L_TA), LR(L_TA), LR(L_RMS))
                    e.tensor_scalar_max(LR(L_TA), LR(L_TA), 1e-4)
                    return e.reciprocal(LR(L_TB), LR(L_TA))
                S.op("vector", krow_a, wait=[("sr2", S.val("sr2")),
                                              ("tr2", S.val("tr2"))],
                     inc=[("vr4", 1)])
                def krow_b(e, ca=ca, rbeta=rbeta):
                    e.tensor_scalar_mul(LR(L_K), LR(L_TB), 127.0)
                    e.tensor_scalar_mul(LR(L_TA), LR(L_TA), -1.0 / 127.0)
                    return e.tensor_scalar_mul(PR(rbeta), LR(L_TA), C(ca))
                S.op("vector", krow_b, wait=[("vr4", S.val("vr4"))],
                     inc=[("vr4", 1)], tag=f"krow{rm2}")
                for s4 in range(4):
                    S.op("tensor", lambda e, s4=s4: e.matmul(
                        pbc[:], onesr[:],
                        lrows[0:1, L_K * NT + s4 * 512:L_K * NT + (s4 + 1) * 512],
                        start=True, stop=True),
                        wait=[("vr4", S.val("vr4")), ("vr5", S.val("vr5"))],
                        inc=[("tr2", 1)])
                    S.op("vector", lambda e, kdst=kdst, s4=s4: e.tensor_copy(
                        kdst[:, s4 * 512:(s4 + 1) * 512], pbc[:]),
                        wait=[("tr2", S.val("tr2"))], inc=[("vr5", 1)])
            for s4 in range(4):
                S.op("tensor", lambda e, s4=s4: e.matmul(
                    pbc[:], onesr[:],
                    prows[0:1, PA * NT + s4 * 512:PA * NT + (s4 + 1) * 512],
                    start=True, stop=True),
                    wait=[("vr4", S.val("vr4")), ("vr5", S.val("vr5"))],
                    inc=[("tr2", 1)])
                S.op("vector", lambda e, s4=s4: e.tensor_copy(
                    b1bc[:, s4 * 512:(s4 + 1) * 512], pbc[:]),
                    wait=[("tr2", S.val("tr2"))], inc=[("vr5", 1)])
            kbc_done = S.val("vr5")
            xstg5 = [SB(p5, f"xstg5{i}", [128, NT]) for i in range(2)]
            vq_t = SB(p5, "vq_t", [128, NT])
            yi_t = SB(p5, "yi_t", [128, NT], I32)
            cp_t = SB(p5, "cp_t", [128, NT])
            cm_t = SB(p5, "cm_t", [128, NT])
            ybt = SB(p5, "ybt", [128, NT], BF16)
            for dc in range(16):
                bi = dc % 2
                S.dma("sync", xstg5[bi][:], xhat_hbm[dc * 128:(dc + 1) * 128, :],
                      wait=[("vqz", max(0, S.val("vqz") - 12)),
                            ("gxh0", S.val("gxh0")), ("gxh1", S.val("gxh1"))],
                      inc=[("lx5", 16)])
                for which, (gt, kbct) in enumerate([(g1t, k1bc), (g2t, k2bc)]):
                    S.op("vector", lambda e, bi=bi, dc=dc, gt=gt, kbct=kbct:
                         e.scalar_tensor_tensor(vq_t[:], xstg5[bi][:],
                                                gt[:, dc:dc + 1], kbct[:],
                                                ALU.mult, ALU.mult),
                         wait=[("lx5", S.val("lx5")), ("vr5", kbc_done)],
                         inc=[("vqz", 1)], tag=f"qv{dc}_{which}")
                    S.op("vector", lambda e: e.tensor_copy(yi_t[:], vq_t[:]),
                         inc=[("vqz", 1)])
                    S.op("vector", lambda e: e.scalar_tensor_tensor(
                        cp_t[:], yi_t[:], 0.5, vq_t[:], ALU.add, ALU.is_lt),
                        inc=[("vqz", 1)])
                    S.op("vector", lambda e: e.scalar_tensor_tensor(
                        cm_t[:], yi_t[:], -0.5, vq_t[:], ALU.add, ALU.is_gt),
                        inc=[("vqz", 1)])
                    S.op("vector", lambda e: e.scalar_tensor_tensor(
                        cp_t[:], cm_t[:], -1.0, cp_t[:], ALU.mult, ALU.add),
                        inc=[("vqz", 1)])
                    S.op("vector", lambda e: e.scalar_tensor_tensor(
                        ybt[:], yi_t[:], 1.0, cp_t[:], ALU.mult, ALU.add),
                        wait=[("gy", S.val("gy"))],
                        inc=[("vqz", 1)], tag=f"qy{dc}_{which}")
                    S.dma("gpsimd", y_hbm[which, dc * 128:(dc + 1) * 128, :],
                          ybt[:], wait=[("vqz", S.val("vqz"))], inc=[("gy", 16)])
        p5_done = [("gy", S.val("gy"))]
        S.barrier()

        if DEBUG and phases == 6:
            with contextlib.ExitStack() as pd:
                ydf = SB(pd, "ydf", [128, NT], BF16)
                ydf2 = SB(pd, "ydf2", [128, NT])
                for which, dk in [(0, "y1"), (1, "y2")]:
                    for dc in range(16):
                        S.dma("sync", ydf[:],
                              y_hbm[which, dc * 128:(dc + 1) * 128, :],
                              wait=p5_done + [("vd", S.val("vd"))],
                              inc=[("od", 16)])
                        S.op("vector", lambda e: e.tensor_copy(ydf2[:], ydf[:]),
                             wait=[("od", S.val("od"))], inc=[("vd", 1)])
                        S.dma("sync", dbg[dk][dc * 128:(dc + 1) * 128, :],
                              ydf2[:], wait=[("vd", S.val("vd"))],
                              inc=[("od", 16)])
                S.dma("sync", dbg["rows"][2:3, :], PR(PA), wait=p5_done,
                      inc=[("od", 16)])
                S.dma("sync", dbg["rows"][3:4, :], PR(PC), inc=[("od", 16)])
                S.dma("sync", dbg["rows"][4:5, :], k1bc[0:1, :], inc=[("od", 16)])
                S.dma("sync", dbg["rows"][5:6, :], k2bc[0:1, :], inc=[("od", 16)])
                for dc in range(16):
                    S.dma("sync", dbg["xhat"][dc * 128:(dc + 1) * 128, :],
                          xhat_hbm[dc * 128:(dc + 1) * 128, :],
                          wait=[("od", S.val("od"))], inc=[("od", 16)])
                S.dma("sync", dbg["srow"][0:1, 0:4], cells[0:1, 0:4],
                      inc=[("od", 16)])
                S.dma("sync", dbg["srow"][1:2, 0:4], cells[0:1, 4:8],
                      inc=[("od", 16)])
                S.dma("sync", dbg["srow"][2:3, 0:4], csnap[0:1, 3:7],
                      inc=[("od", 16)])
                S.dma("sync", dbg["srow"][3:4, 0:4], csnap[0:1, 19:23],
                      inc=[("od", 16)])
                S.dma("sync", out[0:128, :], xq[0:128, :], inc=[("od", 16)])
                S.op("sync", lambda e: e.nop(), wait=[("od", S.val("od"))],
                     inc=[("done", 1)])
            fctx.close(); prctx.close()
            with nc.Block() as block:
                S.emit(block)
            return nc

        with contextlib.ExitStack() as p6:
            y1b = SB(p6, "y1b", [128, 16 * NT], BF16)
            y2b = SB(p6, "y2b", [128, 16 * NT], BF16)
            for which, yb in [(0, y1b), (1, y2b)]:
                for dc in range(16):
                    S.dma("sync", yb[:, dc * NT:(dc + 1) * NT],
                          y_hbm[which, dc * 128:(dc + 1) * 128, :],
                          wait=p5_done, inc=[("ly", 16)])
            y_ld = S.val("ly")
            w1s = SB(p6, "w1s", [128, 2048])
            w2s = SB(p6, "w2s", [128, 2048])
            t1b = [SB(p6, f"t1b{i}", [128, 2048], BF16) for i in range(2)]
            t2b = [SB(p6, f"t2b{i}", [128, 2048], BF16) for i in range(2)]
            tf = SB(p6, "tf", [128, 512])
            psib = [SB(p6, f"psib{i}", [128, 512], BF16) for i in range(2)]
            sqb3 = SB(p6, "sqb3", [128, 512], BF16)
            pu1 = [PS(p6, f"pu1{i}", [128, 512]) for i in range(2)]
            pu2 = [PS(p6, f"pu2{i}", [128, 512]) for i in range(2)]
            pst3 = [PS(p6, f"pst3{i}", [1, 512]) for i in range(4)]
            S.op("vector", lambda e: e.memset(m2r3[:], 0.0), inc=[("vb1", 1)])
            b1_done = S.val("vb1")
            tu0 = S.val("tu")
            tss0 = S.val("tss")
            vm30 = S.val("vm3")
            ssq_after = {}
            pend_stat = None
            def emit_stat():
                nonlocal pend_stat
                if pend_stat is None:
                    return
                gsl, sl, par = pend_stat
                hcp = gsl // 4
                S.op("tensor", lambda e, sl=sl, hcp=hcp: e.matmul(
                    pst3[sl][0:1, :], onesb[:], sqb3[:],
                    start=(hcp == 0), stop=(hcp == NHC - 1)),
                    wait=[("ssq", ssq_after[gsl])], inc=[("tss", 1)],
                    tag=f"st3_{gsl}")
                pend_stat = None
            for hc in range(NHC):
                tp = hc % 2
                S.dma("sync", w1s[:], w1t[hc],
                      wait=[("vt12", S.val("vt12"))], inc=[("lw1", 16)])
                S.dma("sync", w2s[:], w2t[hc], inc=[("lw1", 16)])
                tern_wait = [("lw1", S.val("lw1"))]
                if hc >= 2:
                    tern_wait.append(("tu", tu0 + 8 * (hc - 1)))
                if hc == 0:
                    tern_wait += p2_done
                S.op("vector", lambda e, tp=tp: e.tensor_scalar(
                    t1b[tp][:], w1s[:], thcol[:, 0:1], None, ALU.is_ge),
                    wait=tern_wait, inc=[("vt12", 1)])
                S.op("vector", lambda e, tp=tp: e.scalar_tensor_tensor(
                    t1b[tp][:], w1s[:], thcol[:, 1:2], t1b[tp][:], ALU.is_lt,
                    ALU.subtract), inc=[("vt12", 1)])
                S.op("vector", lambda e, tp=tp: e.tensor_scalar(
                    t2b[tp][:], w2s[:], thcol[:, 2:3], None, ALU.is_ge),
                    inc=[("vt12", 1)])
                S.op("vector", lambda e, tp=tp: e.scalar_tensor_tensor(
                    t2b[tp][:], w2s[:], thcol[:, 3:4], t2b[tp][:], ALU.is_lt,
                    ALU.subtract), inc=[("vt12", 1)], tag=f"tern{hc}")
                vt_hc = S.val("vt12")
                for sl in range(4):
                    gsl = hc * 4 + sl
                    sbi = sl % 2
                    par = gsl % 2
                    def u1mm(e, sbi=sbi, sl=sl, tp=tp):
                        last = None
                        for dc in range(16):
                            last = e.matmul(
                                pu1[sbi][:], t1b[tp][:, dc * 128:(dc + 1) * 128],
                                y1b[:, dc * NT + sl * 512:dc * NT + (sl + 1) * 512],
                                start=(dc == 0), stop=(dc == 15))
                        return last
                    S.op("tensor", u1mm,
                         wait=[("vt12", vt_hc), ("ly", y_ld),
                               ("vu1", max(0, S.val("vu1") - 1))],
                         inc=[("tu", 1)], tag=f"u1mm{hc}_{sl}")
                    def u2mm(e, sbi=sbi, sl=sl, tp=tp):
                        last = None
                        for dc in range(16):
                            last = e.matmul(
                                pu2[sbi][:], t2b[tp][:, dc * 128:(dc + 1) * 128],
                                y2b[:, dc * NT + sl * 512:dc * NT + (sl + 1) * 512],
                                start=(dc == 0), stop=(dc == 15))
                        return last
                    S.op("tensor", u2mm,
                         wait=[("vps", max(0, S.val("vps") - 1))],
                         inc=[("tu", 1)], tag=f"u2mm{hc}_{sl}")
                    emit_stat()
                    S.op("vector", lambda e, sbi=sbi, sl=sl: e.tensor_mul(
                        tf[:], pu1[sbi][:], b1bc[:, sl * 512:(sl + 1) * 512]),
                        wait=[("tu", tu0 + 2 * gsl + 1), ("vps", S.val("vps"))],
                        inc=[("vu1", 1)], tag=f"bt{hc}_{sl}")
                    S.op("scalar", lambda e: e.activation(tf[:], tf[:], AF.Silu),
                         wait=[("vu1", S.val("vu1"))], inc=[("ssw", 1)])
                    psi_wait = [("ssw", S.val("ssw")), ("tu", tu0 + 2 * gsl + 2),
                                (f"gps{par}", S.val(f"gps{par}"))]
                    if gsl >= 2:
                        psi_wait.append(("ssq", ssq_after[gsl - 2]))
                    S.op("vector", lambda e, sbi=sbi, par=par: e.tensor_mul(
                        psib[par][:], tf[:], pu2[sbi][:]),
                        wait=psi_wait, inc=[("vps", 1)], tag=f"psi{hc}_{sl}")
                    vps_sl = S.val("vps")
                    S.dma("gpsimd", psi_hbm[sl, hc * 128:(hc + 1) * 128, :],
                          psib[par][:],
                          wait=[("vps", vps_sl)], inc=[(f"gps{par}", 16)])
                    sq_wait = [("vps", vps_sl)]
                    if gsl >= 1:
                        sq_wait += [("tss", tss0 + gsl), ("vm3", vm30 + gsl)]
                    S.op("scalar", lambda e, par=par: e.activation(
                        sqb3[:], psib[par][:],
                        AF.Square), wait=sq_wait, inc=[("ssq", 1)])
                    ssq_after[gsl] = S.val("ssq")
                    pend_stat = (gsl, sl, par)
                    m2_wait = [("ssq", S.val("ssq"))]
                    if gsl == 0:
                        m2_wait.append(("vb1", b1_done))
                    S.op("vector", lambda e, hc=hc, sl=sl:
                         e.scalar_tensor_tensor(
                             m2r3[:, sl * 512:(sl + 1) * 512], sqb3[:],
                             g3sq[:, hc:hc + 1],
                             m2r3[:, sl * 512:(sl + 1) * 512],
                             ALU.mult, ALU.max),
                         wait=m2_wait, inc=[("vm3", 1)], tag=f"m3{hc}_{sl}")
            emit_stat()
            tss_all = S.val("tss")
            for sl in range(4):
                S.op("vector", lambda e, sl=sl: e.tensor_copy(
                    prows[0:1, PB * NT + sl * 512:PB * NT + (sl + 1) * 512],
                    pst3[sl][0:1, :]),
                    wait=[("tss", tss_all)], inc=[("vsr", 1)])
        p6_done = [(f"gps{i}", S.val(f"gps{i}")) for i in range(2)] + [
            ("vsr", S.val("vsr")), ("vm3", S.val("vm3"))]
        S.barrier()

        if DEBUG and phases == 7:
            with contextlib.ExitStack() as pd:
                S.dma("sync", dbg["rows"][0:1, :], PR(PB),
                      wait=p6_done + [("od", S.val("od"))], inc=[("od", 16)])
                mdf = SB(pd, "mdf", [128, NT])
                S.op("vector", lambda e: e.tensor_copy(mdf[:], m2r3[:]),
                     wait=p6_done, inc=[("vd", 1)])
                S.dma("sync", dbg["x2"][0:128, :], mdf[:],
                      wait=[("vd", S.val("vd"))], inc=[("od", 16)])
                ydf7 = SB(pd, "ydf7", [128, 512], BF16)
                ydf7f = SB(pd, "ydf7f", [128, 512])
                for sl4 in range(4):
                    S.dma("sync", ydf7[:],
                          psi_hbm[sl4, 0:128, :],
                          wait=[("vd", S.val("vd"))], inc=[("od", 16)])
                    S.op("vector", lambda e, sl4=sl4: e.tensor_copy(
                        ydf7f[:], ydf7[:]),
                        wait=[("od", S.val("od"))], inc=[("vd", 1)])
                    S.dma("sync", dbg["x2"][128:256, sl4 * 512:(sl4 + 1) * 512],
                          ydf7f[:], wait=[("vd", S.val("vd"))], inc=[("od", 16)])
                S.dma("sync", out[0:128, :], xq[0:128, :], inc=[("od", 16)])
                S.op("sync", lambda e: e.nop(), wait=[("od", S.val("od"))],
                     inc=[("done", 1)])
            fctx.close(); prctx.close()
            with nc.Block() as block:
                S.emit(block)
            return nc

        with contextlib.ExitStack() as p7:
            lr7 = SB(p7, "lr7", [1, 3 * NT])
            L7 = lambda i: lr7[0:1, i * NT:(i + 1) * NT]
            def hrows1(e):
                e.tensor_mul(L7(0), PR(PC), PR(PC))
                e.tensor_mul(L7(1), PR(PB), L7(0))
                return e.tensor_scalar(L7(1), L7(1), 1.0 / H, 1e-8, ALU.mult,
                                       ALU.add)
            S.op("vector", hrows1, wait=p6_done, inc=[("vr7", 1)])
            S.op("scalar", lambda e: e.activation(L7(1), L7(1), AF.Ln),
                 wait=[("vr7", S.val("vr7"))], inc=[("sr7", 1)])
            S.op("scalar", lambda e: e.activation(L7(1), L7(1), AF.Exp,
                                                  scale=-0.5), inc=[("sr7", 1)])
            ptm3 = PS(p7, "ptm3", [128, 128], BF16)
            mx_tok3 = SB(p7, "mx_tok3", [128, 16], BF16)
            for tcn in range(16):
                S.op("tensor", lambda e, tcn=tcn: e.transpose(
                    ptm3[:], m2r3[:, tcn * 128:(tcn + 1) * 128], idb[:]),
                    wait=p6_done + [("vx7", S.val("vx7"))], inc=[("tx7", 1)])
                S.op("vector", lambda e, tcn=tcn: e.reduce_max(
                    mx_tok3[:, tcn:tcn + 1], ptm3[:], axis=AX.X),
                    wait=[("tx7", S.val("tx7"))], inc=[("vx7", 1)])
            S.dma("gpsimd", m2x_hbm[:], mx_tok3[:],
                  wait=[("vx7", S.val("vx7"))], inc=[("gx7", 16)])
            S.op("gpsimd", lambda e: e.dma_start(
                out=L7(2), in_=m2x_hbm.rearrange("p c -> c p"),
                allow_slow_non_contiguous=True),
                wait=[("gx7", S.val("gx7"))], inc=[("gx7", 16)])
            S.op("scalar", lambda e: e.activation(L7(2), L7(2), AF.Ln),
                 wait=[("gx7", S.val("gx7")), ("sr7", S.val("sr7"))],
                 inc=[("sr7", 1)])
            S.op("scalar", lambda e: e.activation(L7(2), L7(2), AF.Exp,
                                                  scale=0.5), inc=[("sr7", 1)])
            def hrows2a(e):
                e.tensor_scalar_mul(L7(0), PR(PC), -1.0)
                e.tensor_mul(L7(2), L7(2), L7(0))
                e.tensor_mul(L7(2), L7(2), L7(1))
                e.tensor_scalar_max(L7(2), L7(2), 1e-4)
                e.tensor_mul(L7(0), L7(0), L7(1))
                return e.reciprocal(PR(PD), L7(2))
            S.op("vector", hrows2a, wait=[("sr7", S.val("sr7"))], inc=[("vr7", 1)])
            def hrows2b(e):
                e.tensor_scalar_mul(PR(PD), PR(PD), 127.0)
                e.tensor_mul(L7(0), L7(0), PR(PD))
                e.tensor_scalar_mul(PR(PC), L7(0), -1.0)
                e.tensor_scalar_mul(L7(2), L7(2), -1.0 / 127.0)
                return e.tensor_scalar_mul(PR(PD), L7(2), C(C_A3))
            S.op("vector", hrows2b, wait=[("vr7", S.val("vr7"))], inc=[("vr7", 1)])
        p7_done = S.val("vr7")
        S.barrier()
        fctx.close()

        with contextlib.ExitStack() as p8:
            k3bc = SB(p8, "k3bc", [128, NT]); b3bc = SB(p8, "b3bc", [128, NT])
            pbc3 = PS(p8, "pbc3", [128, 512])
            for ridx, dst in [(PC, k3bc), (PD, b3bc)]:
                for s4 in range(4):
                    S.op("tensor", lambda e, ridx=ridx, s4=s4: e.matmul(
                        pbc3[:], onesr[:],
                        prows[0:1, ridx * NT + s4 * 512:ridx * NT + (s4 + 1) * 512],
                        start=True, stop=True),
                        wait=[("vr7", p7_done), ("vb3", S.val("vb3"))],
                        inc=[("tb3", 1)])
                    S.op("vector", lambda e, dst=dst, s4=s4: e.tensor_copy(
                        dst[:, s4 * 512:(s4 + 1) * 512], pbc3[:]),
                        wait=[("tb3", S.val("tb3"))], inc=[("vb3", 1)])
            b3_done = S.val("vb3")
            S.op("vector", lambda e: e.memset(PR(PA), 0.0), inc=[("vb3", 1)])
            S.op("vector", lambda e: e.memset(PR(PB), 0.0), inc=[("vb3", 1)])
            pa_init = S.val("vb3")
            t80 = S.val("t8")
            v8x0 = S.val("v8x")
            s80 = S.val("s8")
            t8s0 = S.val("t8s")
            v8s0 = S.val("v8s")
            gx20 = S.val("gx2")
            l8x0 = S.val("l8x")
            y3T = [SB(p8, f"y3T{i}", [128, NHC * 512], BF16) for i in range(2)]
            psg = [SB(p8, f"psg{i}", [128, 512], BF16) for i in range(2)]
            t3s = SB(p8, "t3s", [128, 8192], BF16)
            xh8 = SB(p8, "xh8", [128, 512])
            vq8 = SB(p8, "vq8", [128, 512])
            vq9 = vq8
            yi8 = SB(p8, "yi8", [128, 512], I32)
            cp8 = SB(p8, "cp8", [128, 512], BF16)
            cm8 = SB(p8, "cm8", [128, 512], BF16)
            sq8 = SB(p8, "sq8", [128, 512], BF16)
            x2w = [SB(p8, f"x2w{i}", [128, 512], BF16) for i in range(2)]
            pw3 = [PS(p8, f"pw3{i}", [128, 512]) for i in range(2)]
            pst8 = [PS(p8, f"pst8{i}", [1, 512]) for i in range(2)]

            def emit_quant(T, hc):
                bi = hc % 2
                Tp = T % 2
                S.dma("gpsimd", psg[bi][:],
                      psi_hbm[T, hc * 128:(hc + 1) * 128, :],
                      wait=[("v8q", max(0, S.val("v8q") - 11))],
                      inc=[("g8", 16)])
                S.op("vector", lambda e, bi=bi, hc=hc, T=T:
                     e.scalar_tensor_tensor(
                         vq8[:], psg[bi][:], g3t[:, hc:hc + 1],
                         k3bc[:, T * 512:(T + 1) * 512], ALU.mult, ALU.mult),
                     wait=[("g8", S.val("g8")), ("vb3", b3_done)],
                     inc=[("v8q", 1)], tag=f"q3v{T}_{hc}")
                S.op("vector", lambda e: e.tensor_copy(yi8[:], vq8[:]),
                     inc=[("v8q", 1)])
                S.op("vector", lambda e: e.scalar_tensor_tensor(
                    cp8[:], yi8[:], 0.5, vq8[:], ALU.add, ALU.is_lt),
                    inc=[("v8q", 1)])
                S.op("vector", lambda e: e.scalar_tensor_tensor(
                    cm8[:], yi8[:], -0.5, vq8[:], ALU.add, ALU.is_gt),
                    inc=[("v8q", 1)])
                S.op("vector", lambda e: e.scalar_tensor_tensor(
                    cp8[:], cm8[:], -1.0, cp8[:], ALU.mult, ALU.add),
                    inc=[("v8q", 1)])
                qy_wait = [("t8", t80 + 32 * (T - 1))] if T >= 2 else []
                S.op("vector", lambda e, hc=hc, Tp=Tp: e.scalar_tensor_tensor(
                    y3T[Tp][:, hc * 512:(hc + 1) * 512], yi8[:], 1.0, cp8[:],
                    ALU.mult, ALU.add),
                    wait=qy_wait, inc=[("v8q", 1)], tag=f"q3y{T}_{hc}")

            for hc in range(NHC):
                emit_quant(0, hc)
            v8q_after_T = {0: S.val("v8q")}
            pend8 = None

            def emit_stat8(T):
                nonlocal pend8
                if pend8 is None:
                    return
                gm, oc = pend8, pend8 % 16
                sa_wait = [("v8x", v8x0 + 2 * gm + 2)]
                sb_wait = [("s8", s80 + gm + 1)]
                if oc == 0 and T > 0:
                    sa_wait.append(("v8s", v8s0 + 2 * T))
                    sb_wait.append(("v8s", v8s0 + 2 * T))
                S.op("tensor", lambda e, gm=gm, oc=oc: e.matmul(
                    pst8[0][0:1, :], onesb[:], x2w[gm % 2][:],
                    start=(oc == 0), stop=(oc == 15)),
                    wait=sa_wait, inc=[("t8s", 1)], tag=f"stA{gm}")
                S.op("tensor", lambda e, oc=oc: e.matmul(
                    pst8[1][0:1, :], onesb[:], sq8[:],
                    start=(oc == 0), stop=(oc == 15)),
                    wait=sb_wait, inc=[("t8s", 1)], tag=f"stB{gm}")
                pend8 = None

            for T in range(4):
                for oc in range(16):
                    gm = T * 16 + oc
                    obi = oc % 2
                    Tp = T % 2
                    lo_wait = [("t8", t80 + 2 * gm - 1)] if gm > 0 else []
                    hi_wait = [("t8", t80 + 2 * gm)] if gm > 0 else []
                    S.dma("sync", t3s[:, 0:4096], t3_hbm[oc, :, 0:4096],
                          wait=lo_wait, inc=[("l83", 16)])
                    l83_lo = S.val("l83")
                    S.dma("sync", t3s[:, 4096:8192], t3_hbm[oc, :, 4096:8192],
                          wait=hi_wait, inc=[("l83", 16)])
                    l83_hi = S.val("l83")

                    def mmlo(e, obi=obi, Tp=Tp):
                        last = None
                        for hcc in range(32):
                            last = e.matmul(
                                pw3[obi][:], t3s[:, hcc * 128:(hcc + 1) * 128],
                                y3T[Tp][:, hcc * 512:(hcc + 1) * 512],
                                start=(hcc == 0), stop=False)
                        return last
                    S.op("tensor", mmlo,
                         wait=[("l83", l83_lo), ("v8q", v8q_after_T[T])] +
                              ([("v8x", v8x0 + 2 * gm - 3)] if gm >= 2 else []),
                         inc=[("t8", 1)], tag=f"w3lo{T}_{oc}")

                    def mmhi(e, obi=obi, Tp=Tp):
                        last = None
                        for hcc in range(32, 64):
                            last = e.matmul(
                                pw3[obi][:], t3s[:, hcc * 128:(hcc + 1) * 128],
                                y3T[Tp][:, hcc * 512:(hcc + 1) * 512],
                                start=False, stop=(hcc == 63))
                        return last
                    S.op("tensor", mmhi, wait=[("l83", l83_hi)],
                         inc=[("t8", 1)], tag=f"w3hi{T}_{oc}")
                    emit_stat8(T)

                    S.dma("sync", xh8[:],
                          xhat_hbm[oc * 128:(oc + 1) * 128,
                                   T * 512:(T + 1) * 512],
                          wait=([("v8x", v8x0 + 2 * gm)] if gm > 0 else []),
                          inc=[("l8x", 16)])
                    S.op("vector", lambda e, obi=obi, T=T: e.tensor_mul(
                        vq9[:], pw3[obi][:], b3bc[:, T * 512:(T + 1) * 512]),
                        wait=[("t8", t80 + 2 * gm + 2),
                              ("l8x", l8x0 + 16 * (gm + 1))],
                        inc=[("v8x", 1)], tag=f"bx{T}_{oc}")
                    add_wait = ([("s8", s80 + gm - 1),
                                 ("gx2", gx20 + 16 * (gm - 1))] if gm >= 1
                                else []) + \
                               ([("t8s", t8s0 + 2 * gm - 3)] if gm >= 2 else [])
                    S.op("vector", lambda e, obi=obi: e.tensor_add(
                        x2w[obi][:], vq9[:], xh8[:]),
                        wait=add_wait, inc=[("v8x", 1)], tag=f"x2{T}_{oc}")
                    S.dma("gpsimd",
                          x2_hbm[oc * 128:(oc + 1) * 128, T * 512:(T + 1) * 512],
                          x2w[obi][:], wait=[("v8x", v8x0 + 2 * gm + 2)],
                          inc=[("gx2", 16)])
                    sq_wait8 = [("v8x", v8x0 + 2 * gm + 2)]
                    if gm >= 1:
                        sq_wait8.append(("t8s", t8s0 + 2 * gm))
                    S.op("scalar", lambda e, obi=obi: e.activation(
                        sq8[:], x2w[obi][:], AF.Square),
                        wait=sq_wait8, inc=[("s8", 1)])
                    pend8 = gm
                    if T < 3:
                        for hh in range(4):
                            emit_quant(T + 1, 4 * oc + hh)
                emit_stat8(T)
                if T < 3:
                    v8q_after_T[T + 1] = S.val("v8q")
                ra_wait = [("t8s", t8s0 + 32 * (T + 1))]
                if T == 0:
                    ra_wait.append(("vb3", pa_init))
                for which, ridx in [(0, PA), (1, PB)]:
                    S.op("vector", lambda e, which=which, ridx=ridx, T=T:
                         e.scalar_tensor_tensor(
                             prows[0:1, ridx * NT + T * 512:
                                   ridx * NT + (T + 1) * 512],
                             pst8[which][0:1, :], 1.0,
                             prows[0:1, ridx * NT + T * 512:
                                   ridx * NT + (T + 1) * 512],
                             ALU.mult, ALU.add),
                         wait=ra_wait, inc=[("v8s", 1)], tag=f"l2s{T}_{which}")
        p8_done = [("v8s", S.val("v8s")), ("v8x", S.val("v8x")),
                   ("gx2", S.val("gx2"))]
        S.barrier()

        if DEBUG and phases == 8:
            with contextlib.ExitStack() as pd:
                S.dma("sync", dbg["rows"][0:1, :], PR(PA),
                      wait=p8_done + [("od", S.val("od"))], inc=[("od", 16)])
                S.dma("sync", dbg["rows"][1:2, :], PR(PB), inc=[("od", 16)])
                xdf8 = SB(pd, "xdf8", [128, NT], BF16)
                xdf8f = SB(pd, "xdf8f", [128, NT])
                for dc in range(16):
                    S.dma("sync", xdf8[:], x2_hbm[dc * 128:(dc + 1) * 128, :],
                          wait=[("vd", S.val("vd"))], inc=[("od", 16)])
                    S.op("vector", lambda e: e.tensor_copy(xdf8f[:], xdf8[:]),
                         wait=[("od", S.val("od"))], inc=[("vd", 1)])
                    S.dma("sync", dbg["x2"][dc * 128:(dc + 1) * 128, :],
                          xdf8f[:], wait=[("vd", S.val("vd"))], inc=[("od", 16)])
                S.dma("sync", out[0:128, :], xq[0:128, :], inc=[("od", 16)])
                S.op("sync", lambda e: e.nop(), wait=[("od", S.val("od"))],
                     inc=[("done", 1)])
            prctx.close()
            with nc.Block() as block:
                S.emit(block)
            return nc

        with contextlib.ExitStack() as p9:
            mbc2 = SB(p9, "mbc2", [128, NT]); rbc2 = SB(p9, "rbc2", [128, NT])
            pbc4 = PS(p9, "pbc4", [128, 512])
            def ln2rows(e):
                e.tensor_scalar_mul(PR(PA), PR(PA), 1.0 / E)
                e.tensor_scalar_mul(PR(PB), PR(PB), 1.0 / E)
                e.tensor_mul(PR(PC), PR(PA), PR(PA))
                e.tensor_sub(PR(PB), PR(PB), PR(PC))
                return e.tensor_scalar_add(PR(PB), PR(PB), 1e-5)
            S.op("vector", ln2rows, wait=p8_done, inc=[("vr9", 1)])
            S.op("scalar", lambda e: e.activation(PR(PB), PR(PB), AF.Ln),
                 wait=[("vr9", S.val("vr9"))], inc=[("sr9", 1)])
            S.op("scalar", lambda e: e.activation(PR(PB), PR(PB), AF.Exp,
                                                  scale=-0.5), inc=[("sr9", 1)])
            for ridx, dst in [(PA, mbc2), (PB, rbc2)]:
                for s4 in range(4):
                    S.op("tensor", lambda e, ridx=ridx, s4=s4: e.matmul(
                        pbc4[:], onesr[:],
                        prows[0:1, ridx * NT + s4 * 512:ridx * NT + (s4 + 1) * 512],
                        start=True, stop=True),
                        wait=[("sr9", S.val("sr9")), ("vr10", S.val("vr10"))],
                        inc=[("tr9", 1)])
                    S.op("vector", lambda e, dst=dst, s4=s4: e.tensor_copy(
                        dst[:, s4 * 512:(s4 + 1) * 512], pbc4[:]),
                        wait=[("tr9", S.val("tr9"))], inc=[("vr10", 1)])
            bc9 = S.val("vr10")
            of = [SB(p9, f"of{i}", [128, NT]) for i in range(2)]
            xf9 = [SB(p9, f"xf9{i}", [128, NT], BF16) for i in range(2)]
            for oc in range(16):
                bi = oc % 2
                S.dma("sync", xf9[bi][:], x2_hbm[oc * 128:(oc + 1) * 128, :],
                      wait=[("vo", max(0, S.val("vo") - 3))], inc=[("lx9", 16)])
                S.op("vector", lambda e, bi=bi: e.tensor_sub(
                    of[bi][:], xf9[bi][:], mbc2[:]),
                    wait=[("vr10", bc9), ("lx9", S.val("lx9")),
                          (f"go{bi}", S.val(f"go{bi}"))],
                    inc=[("vo", 1)])
                S.op("vector", lambda e, bi=bi: e.tensor_mul(
                    of[bi][:], of[bi][:], rbc2[:]), inc=[("vo", 1)],
                    tag=f"out{oc}")
                S.dma("sync", out[oc * 128:(oc + 1) * 128, :], of[bi][:],
                      wait=[("vo", S.val("vo"))], inc=[(f"go{bi}", 16)])
            S.op("sync", lambda e: e.nop(),
                 wait=[("go0", S.val("go0")), ("go1", S.val("go1"))],
                 inc=[("done", 1)])
        prctx.close()

        with nc.Block() as block:
            S.emit(block)
    return nc


_CACHE = {}


def kernel(**inputs):
    in_maps = prep_inputs(inputs)
    if "nc" not in _CACHE:
        _CACHE["nc"] = build()
    res = run_bass_kernel_spmd(_CACHE["nc"], in_maps, core_ids=list(range(8)))
    outp = np.empty((B, L, E), dtype=np.float32)
    for c in range(8):
        b, par = c >> 1, c & 1
        qcols = np.concatenate([np.arange(g * 512, (g + 1) * 512)
                                for g in Q_TILES[par]])
        outp[b, qcols, :] = np.asarray(res.results[c]["out"]).T
    return outp
